# version 33
# baseline (speedup 1.0000x reference)
import numpy as np
import ml_dtypes
import concourse.bass as bass
import concourse.mybir as mybir
from concourse.bass_utils import run_bass_kernel_spmd

F32 = mybir.dt.float32
BF16 = mybir.dt.bfloat16
I32 = mybir.dt.int32
I8 = mybir.dt.int8
AF = mybir.ActivationFunctionType
ALU = mybir.AluOpType
AX = mybir.AxisListType

T = 2048
D = 1024
FI = 4096
NTT = T // 128
NG = D // 128
NIG = FI // 128
EPS = 1e-5
KB = 1024
HT = 1024
NXT = T + 1


class Sem:
    __slots__ = ("h", "n")

    def __init__(self, h):
        self.h = h
        self.n = 0


class Eng:
    def __init__(self, sem, self_sync=False):
        self.sem = sem
        self.self_sync = self_sync
        self.fns = []
        self.waited = {}

    def wait(self, tok):
        if tok is None:
            return
        sem, val = tok
        if val <= 0 or self.waited.get(id(sem), 0) >= val:
            return
        self.waited[id(sem)] = val
        self.fns.append(lambda e, s=sem, v=val: e.wait_ge(s.h, v))

    def waits(self, toks):
        for t in toks:
            self.wait(t)

    def op(self, fn, inc=1):
        sem = self.sem
        sem.n += inc

        def run(e, f=fn, s=sem, i=inc):
            f(e).then_inc(s.h, i)
        self.fns.append(run)
        tok = (sem, sem.n)
        if self.self_sync:
            self.wait(tok)
        return tok

    def quiet(self, fn):
        if self.self_sync:
            return self.op(lambda e, f=fn: f(e))
        self.fns.append(fn)
        return None

    def gop(self, fn):
        self.wait((self.sem, self.sem.n))
        sem = self.sem
        sem.n += 1

        def run(e, f=fn, s=sem):
            f(e).then_inc(s.h, 1)
        self.fns.append(run)
        tok = (sem, sem.n)
        self.wait(tok)
        return tok


def build_nc(debug=False):
    nc = bass.Bass("TRN2", target_bir_lowering=False,
                   detect_race_conditions=debug)

    x_h = nc.dram_tensor("x", [T, D], F32, kind="ExternalInput")[:]
    out_h = nc.dram_tensor("out", [T, D], F32, kind="ExternalOutput")[:]
    w_h = {}
    for nm, shp in (("wk", [D, D]), ("wv", [D, D]), ("wr", [D, D]),
                    ("wo", [D, D]), ("wg", [FI, D]), ("wu", [FI, D]),
                    ("wd", [D, FI])):
        w_h[nm] = nc.dram_tensor(nm, shp, F32, kind="ExternalInput")[:]
    vec_h = {}
    for nm in ("td", "tf", "tmk", "tmv", "tmr"):
        vec_h[nm] = nc.dram_tensor(nm, [1, D], F32, kind="ExternalInput")[:]
    wgT_s = nc.dram_tensor("wgT_s", [8, 128, FI], BF16, kind="Internal")[:]
    wuT_s = nc.dram_tensor("wuT_s", [8, 128, FI], BF16, kind="Internal")[:]
    wdT_s = nc.dram_tensor("wdT_s", [NIG, 2, 128, 512], BF16, kind="Internal")[:]
    qxT_s = nc.dram_tensor("qxT_s", [NG, 128, T], BF16, kind="Internal")[:]

    ARENA_B = 212800
    BASE = (nc.sbuf_base + 63) & ~63
    esz = {F32: 4, BF16: 2, I32: 4, I8: 1}
    _allocs = {}

    def reg(off_b, nbytes, dt):
        assert off_b % 4 == 0 and off_b + nbytes <= ARENA_B, (off_b, nbytes)
        adt = dt if dt in (F32, BF16) else F32
        key = (off_b, nbytes, adt)
        if key not in _allocs:
            shape = [128, nbytes // esz[adt]]
            _allocs[key] = nc.alloc_sbuf_tensor_at(
                f"rg{off_b}_{nbytes}_{adt}", shape, adt, offset=BASE + off_b)
        h = _allocs[key]
        if dt == adt:
            return h[:, 0:nbytes // esz[dt]]
        return h.bitcast(dt)[:, 0:nbytes // esz[dt]]

    ident = reg(0, 256, BF16)
    ii32 = reg(256, 512, I32)
    ones128f = reg(256, 512, F32)
    smalls = reg(768, 4 * KB, F32)

    def sm(c, n):
        return smalls[:, c:c + n]
    S1, S2 = sm(0, 16), sm(16, 16)
    mtile, ex2 = sm(32, 16), sm(48, 16)
    rstile, nmrs = sm(64, 16), sm(80, 16)
    mxt, tqt = sm(96, 16), sm(112, 16)
    sxt, inv_s = sm(128, 16), sm(144, 16)
    isg, isu = sm(160, 16), sm(176, 16)
    mh_t, tq2 = sm(192, 16), sm(208, 16)
    sh_t, scd = sm(224, 16), sm(240, 16)
    asum = sm(384, 64)
    carryA, carryB = sm(288, 8), sm(296, 8)
    td8, tf8 = sm(304, 8), sm(312, 8)
    ew8, eu8 = sm(320, 8), sm(328, 8)
    tmk8, tmv8, tmr8 = sm(336, 8), sm(344, 8), sm(352, 8)
    ones_col = sm(360, 1)
    mwb = {"wg": sm(361, 1), "wu": sm(362, 1), "wd": sm(363, 1)}
    swb = sm(364, 1)
    mw11 = sm(368, 1)
    rsw11 = sm(369, 1)
    asum_v = sm(370, 1)
    epsb = sm(371, 1)
    mhp = [sm(448, 8), sm(456, 8), sm(464, 8), sm(472, 8)]
    asum_d = sm(480, 8)

    VROW = 4864
    R_XNT = 8960
    R_QX8 = 41856
    R_RWKV = 58240
    R_WT = 91008
    POOL = 156544

    ps = [nc.alloc_psum_tensor(f"psb{i}", [128, 512], F32) for i in range(8)]

    def pa(i, a=0, b=512):
        return ps[i][:, a:b]

    import contextlib
    stack = contextlib.ExitStack()
    nc._keepalive_stack = stack
    sems = {nm: Sem(stack.enter_context(nc.semaphore(nm)))
            for nm in ("gc", "te", "vv", "ss")}
    E = {"sync": Eng(None), "gp": Eng(sems["gc"], self_sync=debug),
         "te": Eng(sems["te"]),
         "v": Eng(sems["vv"], self_sync=True),
         "s": Eng(sems["ss"], self_sync=debug)}
    NQS = 6
    QPOOL = {q: [Sem(stack.enter_context(nc.semaphore(f"{q}{i}")))
                 for i in range(NQS)] for q in ("sync", "gp", "s")}
    QRR = {q: 0 for q in QPOOL}

    def dma(q, out_ap, in_ap, slow=False):
        pool = QPOOL[q]
        s = pool[QRR[q]]
        QRR[q] = (QRR[q] + 1) % NQS
        if s.n > 0:
            E[q].wait((s, s.n))
        s.n += 16

        def run(en, o=out_ap, i=in_ap, ss=s, sl=slow):
            en.dma_start(out=o, in_=i,
                         allow_slow_non_contiguous=sl).then_inc(ss.h, 16)
        E[q].fns.append(run)
        return (s, s.n)

    def barrier():
        toks = [(E[k].sem, E[k].sem.n) for k in E if E[k].sem is not None]
        toks += [(s, s.n) for q in QPOOL for s in QPOOL[q]]
        for k in E:
            E[k].waits(toks)

    def bcast(ap1, n):
        return ap1.broadcast_to((128, n))

    ev_guard = {}

    t_iota = E["gp"].op(lambda e: e.iota(ii32, pattern=[[1, 128]], base=0,
                                         channel_multiplier=-1))
    E["v"].wait(t_iota)
    tok_ident = E["v"].op(lambda e: e.tensor_scalar(ident, ii32, 0.0, None,
                                                    ALU.is_equal))
    tok_ones = E["v"].op(lambda e: e.memset(ones_col, 1.0))
    tok_eps = E["v"].op(lambda e: e.memset(epsb, EPS))

    xf32 = reg(R_WT, 64 * KB, F32)
    tok_x = dma("sync", xf32.rearrange("p (a d) -> p a d", a=NTT),
                x_h.rearrange("(a p) d -> p a d", p=128))

    vdst = {"td": td8, "tf": tf8, "tmk": tmk8, "tmv": tmv8, "tmr": tmr8}
    vtoks = {nm: dma("gp", vdst[nm],
                     vec_h[nm].rearrange("o (g p) -> (o p) g", p=128),
                     slow=True)
             for nm in vdst}

    E["s"].wait(vtoks["td"])
    E["s"].quiet(lambda e: e.activation(ew8, td8, AF.Exp, bias=0.0, scale=1.0))
    tok_ew = E["s"].op(lambda e: e.activation(ew8, ew8, AF.Exp, bias=0.0,
                                              scale=-1.0))
    E["s"].wait(vtoks["tf"])
    tok_eu = E["s"].op(lambda e: e.activation(eu8, tf8, AF.Exp, bias=0.0,
                                              scale=1.0))

    wgE = [reg(R_QX8 + i * 4 * KB, 4 * KB, F32) for i in range(2)]
    wuE = [reg(R_QX8 + 8 * KB + i * 4 * KB, 4 * KB, F32) for i in range(2)]
    absE = reg(VROW, 4 * KB, F32)
    earlyAB = {}

    def passA_early(nm, src_h_m, bufs, q, acol):
        toks = [None] * 32
        earlyAB[nm] = toks
        for rc in range(32):
            if rc >= 2:
                E[q].wait(toks[rc - 2])
            tok_ld = dma(q, bufs[rc % 2], src_h_m[rc * 128:(rc + 1) * 128, :])
            E["s"].wait(tok_ld)
            toks[rc] = E["s"].op(lambda e, rc=rc, b=bufs[rc % 2]: e.activation(
                absE, b, AF.Abs, bias=0.0, scale=1.0,
                accum_out=asum[:, acol + rc:acol + rc + 1]))
            yield

    gA_wg = passA_early("wg", w_h["wg"], wgE, "sync", 0)
    gA_wu = passA_early("wu", w_h["wu"], wuE, "gp", 32)

    def pump(g, n=1):
        for _ in range(n):
            try:
                next(g)
            except StopIteration:
                return
    pump(gA_wu, 6)

    XN_a = reg(R_RWKV, 32 * KB, F32)
    XN_b = reg(R_XNT, 32 * KB, F32)

    def xn_t(tt):
        base = XN_a if tt < 8 else XN_b
        return base[:, (tt % 8) * 1024:(tt % 8 + 1) * 1024]

    def xf_t(tt):
        return xf32[:, tt * 1024:(tt + 1) * 1024]

    sqscr = reg(POOL, 4 * KB, F32)
    E["v"].wait(tok_x)
    E["s"].wait(tok_x)
    last_s = None
    for tt in range(NTT):
        E["v"].gop(lambda e, tt=tt: e.reduce_sum(S1[:, tt:tt + 1], xf_t(tt),
                                                 axis=AX.X))
        last_s = E["s"].op(lambda e, tt=tt: e.activation(
            sqscr, xf_t(tt), AF.Square, bias=0.0, scale=1.0,
            accum_out=S2[:, tt:tt + 1]))
        pump(gA_wg, 1)
        pump(gA_wu, 1)
    E["v"].wait(last_s)
    E["v"].quiet(lambda e: e.tensor_scalar_mul(mtile, S1, 1.0 / D))
    E["v"].quiet(lambda e: e.tensor_scalar_mul(ex2, S2, 1.0 / D))
    E["v"].quiet(lambda e: e.tensor_mul(rstile, mtile, mtile))
    tok_var = E["v"].op(lambda e: e.tensor_sub(ex2, ex2, rstile))
    E["s"].waits([tok_var, tok_eps])
    tok_sd = E["s"].op(lambda e: e.activation(tqt, ex2, AF.Sqrt, bias=epsb,
                                              scale=1.0))
    E["v"].wait(tok_sd)
    E["v"].gop(lambda e: e.reciprocal(rstile, tqt))
    E["v"].quiet(lambda e: e.tensor_mul(nmrs, mtile, rstile))
    tok_ln = E["v"].op(lambda e: e.tensor_scalar_mul(nmrs, nmrs, -1.0))

    tok_mx = None
    for tt in range(NTT):
        E["v"].quiet(lambda e, tt=tt: e.tensor_scalar(
            xn_t(tt), xf_t(tt), rstile[:, tt:tt + 1], nmrs[:, tt:tt + 1],
            ALU.mult, ALU.add))
        tok_mx = E["v"].gop(lambda e, tt=tt: e.reduce_max(
            mxt[:, tt:tt + 1], xn_t(tt), axis=AX.X, apply_absolute_value=True))

    XNB = reg(POOL, 32 * KB, BF16)
    qi32 = reg(POOL + 32 * KB, 4 * KB, I32)
    qxb = [reg(POOL + 36 * KB, 2 * KB, BF16), reg(POOL + 38 * KB, 2 * KB, BF16)]
    qstg = reg(POOL + 40 * KB, 8 * KB, BF16)

    E["v"].wait(tok_mx)
    E["v"].quiet(lambda e: e.tensor_scalar_max(tqt, mxt, EPS))
    E["v"].gop(lambda e: e.reciprocal(sxt, tqt))
    E["v"].quiet(lambda e: e.tensor_scalar_mul(sxt, sxt, 127.0))
    tok_sx = E["v"].op(lambda e: e.tensor_scalar_mul(inv_s, tqt, 1.0 / 127.0))

    E["te"].wait(tok_ident)
    E["s"].wait(tok_sx)
    qst_prev = None
    for tt in range(NTT):
        tok_r = E["s"].op(lambda e, tt=tt: e.activation(
            qi32, xn_t(tt), AF.Copy, bias=0.0, scale=sxt[:, tt:tt + 1]))
        E["v"].wait(tok_r)
        tok_q = E["v"].op(lambda e, tt=tt: e.tensor_copy(qxb[tt % 2], qi32))
        E["s"].wait(tok_q)
        E["v"].op(lambda e, tt=tt: e.tensor_copy(
            XNB[:, tt * 1024:(tt + 1) * 1024], xn_t(tt)))
        E["te"].wait(tok_q)
        rcs = tt % 4
        qv4 = qstg.rearrange("p (dg rr c) -> p dg rr c", rr=4, c=128)
        for half in range(2):
            bank = 4 + half
            E["te"].wait(ev_guard.get(bank))
            tok_mm = None
            for r in range(4):
                dg = half * 4 + r
                tok_mm = E["te"].op(
                    lambda e, dg=dg, r=r, bank=bank, tt=tt: e.matmul(
                        pa(bank, r * 128, (r + 1) * 128),
                        qxb[tt % 2][:, dg * 128:(dg + 1) * 128], ident,
                        start=True, stop=True))
            dstap = qv4[:, half * 4:(half + 1) * 4, rcs:rcs + 1, :]
            E["v"].waits([tok_mm, qst_prev])
            tok_ev = E["v"].op(lambda e, d=dstap, bank=bank:
                               e.tensor_copy(d, pa(bank)))
            ev_guard[bank] = tok_ev
        if rcs == 3:
            E["s"].wait(tok_ev)
            qst_prev = dma("s", qxT_s[:, :, (tt // 4) * 512:
                                       (tt // 4 + 1) * 512]
                           .transpose([1, 0, 2]), qstg)
        pump(gA_wg, 1)
        pump(gA_wu, 1)
    pump(gA_wg, 32)
    pump(gA_wu, 32)
    tok_xnb_last = (E["v"].sem, E["v"].sem.n)
    tok_r_last = (E["s"].sem, E["s"].sem.n)

    xnT = reg(R_XNT, 32896, BF16)

    def xnt_g(g):
        return xnT[:, g * NXT:(g + 1) * NXT]

    for g in range(NG):
        E["v"].quiet(lambda e, g=g: e.memset(xnt_g(g)[:, 0:1], 0.0))
    tok_z0 = E["v"].op(lambda e: e.memset(xnt_g(0)[:, 0:1], 0.0))
    E["s"].wait(tok_z0)
    E["te"].wait(tok_xnb_last)

    ev_flip = [0]

    def trans_batch(src_aps, dst_ap, bank, eng=None, rhs=None):
        if rhs is None:
            rhs = ident
        E["te"].wait(ev_guard.get(bank))
        E["te"].wait(tok_ident)
        tok_mm = None
        for r, s_ap in enumerate(src_aps):
            tok_mm = E["te"].op(lambda e, s_ap=s_ap, r=r, bank=bank, rhs=rhs:
                                e.matmul(pa(bank, r * 128, (r + 1) * 128),
                                         s_ap, rhs, start=True, stop=True))
        n = len(src_aps) * 128
        if eng is None:
            eng = E["s"] if ev_flip[0] % 2 == 0 else E["v"]
            ev_flip[0] += 1
        eng.wait(tok_mm)
        if eng is E["s"]:
            tok_ev = eng.op(lambda e, d=dst_ap, bank=bank, n=n: e.activation(
                d, pa(bank, 0, n), AF.Copy, bias=0.0, scale=1.0))
        else:
            tok_ev = eng.op(lambda e, d=dst_ap, bank=bank, n=n: e.tensor_copy(
                d, pa(bank, 0, n)))
        ev_guard[bank] = tok_ev
        return tok_ev

    for g in range(NG):
        for q in range(4):
            srcs = [XNB[:, (q * 4 + r) * 1024 + g * 128:
                        (q * 4 + r) * 1024 + (g + 1) * 128] for r in range(4)]
            trans_batch(srcs, xnt_g(g)[:, 1 + q * 512:1 + (q + 1) * 512],
                        4 + q % 2)
    tok_xnbtr_mm = (E["te"].sem, E["te"].sem.n)

    WT = {"wk": reg(R_WT, 16 * KB, BF16),
          "wv": reg(R_WT + 16 * KB, 16 * KB, BF16),
          "wr": reg(R_WT + 32 * KB, 16 * KB, BF16)}
    wbuf = [reg(POOL, 16 * KB, BF16), reg(POOL + 16 * KB, 16 * KB, BF16)]

    def transpose_weight(dst, bi, tok_ld):
        E["te"].wait(tok_ld)
        tok = None
        for dg in range(NG):
            for q in range(2):
                srcs = [wbuf[bi][:, (q * 4 + r) * 1024 + dg * 128:
                                 (q * 4 + r) * 1024 + (dg + 1) * 128]
                        for r in range(4)]
                tok = trans_batch(srcs, dst[:, dg * 1024 + q * 512:
                                            dg * 1024 + (q + 1) * 512],
                                  4 + q % 2)
        return tok

    WoT = reg(R_WT + 48 * KB, 16 * KB, BF16)
    E["gp"].wait(tok_xnbtr_mm)
    ld_k = dma("gp", wbuf[0].rearrange("p (a d) -> p a d", a=NG),
               w_h["wk"].rearrange("(a p) d -> p a d", p=128))
    ld_v = dma("gp", wbuf[1].rearrange("p (a d) -> p a d", a=NG),
               w_h["wv"].rearrange("(a p) d -> p a d", p=128))
    tk = transpose_weight(WT["wk"], 0, ld_k)
    E["gp"].wait(tk)
    ld_r = dma("gp", wbuf[0].rearrange("p (a d) -> p a d", a=NG),
               w_h["wr"].rearrange("(a p) d -> p a d", p=128))
    tv = transpose_weight(WT["wv"], 1, ld_v)
    E["gp"].wait(tv)
    ld_o = dma("gp", wbuf[1].rearrange("p (a d) -> p a d", a=NG),
               w_h["wo"].rearrange("(a p) d -> p a d", p=128))
    transpose_weight(WT["wr"], 0, ld_r)
    transpose_weight(WoT, 1, ld_o)
    tok_wtr_evS = ev_guard.get(4)
    tok_wtr_evV = ev_guard.get(5)

    rwkvT = reg(R_RWKV, 32 * KB, BF16)
    XM = {"k": reg(POOL, 16 * KB, BF16),
          "v": reg(POOL + 16 * KB, 16 * KB, BF16),
          "r": reg(POOL + 32 * KB, 16 * KB, BF16)}
    TM = {"k": tmk8, "v": tmv8, "r": tmr8}
    Ta = reg(R_QX8, 4 * KB, F32)
    Tb = reg(R_QX8 + 4 * KB, 4 * KB, F32)
    Tv = reg(R_QX8 + 8 * KB, 4 * KB, F32)
    Tek = reg(R_QX8 + 12 * KB, 4 * KB, F32)
    Tz = reg(POOL + 48 * KB, 4 * KB, F32)
    Td = reg(VROW, 4 * KB, F32)
    TmixT = reg(VROW, 2 * KB, BF16)
    TmixU = reg(VROW + 2 * KB, 2 * KB, BF16)
    Tsd = reg(POOL + 52 * KB, 2 * KB, BF16)

    E["v"].waits([tok_ew, tok_eu, tok_wtr_evS, tok_wtr_evV, tok_r_last,
                  qst_prev])
    if not debug:
        E["v"].self_sync = False
    E["te"].waits([tok_wtr_evS, tok_wtr_evV])
    E["s"].waits([tok_wtr_evS, tok_wtr_evV])
    E["gp"].waits([tok_wtr_evS, tok_wtr_evV, earlyAB["wg"][31],
                   earlyAB["wu"][31], qst_prev])
    pend = [None]

    last_rw = [None]

    def flush_pend():
        if pend[0] is None:
            return
        tok_rd, tok_sr, g, t0 = pend[0]
        pend[0] = None
        E["v"].waits([tok_rd, tok_sr])
        E["v"].quiet(lambda e: e.tensor_mul(Ta, Tv, Td))
        tok_rw = E["v"].op(lambda e, g=g, t0=t0: e.tensor_mul(
            rwkvT[:, g * T + t0:g * T + t0 + HT], Ta, Tsd))
        E["s"].wait(tok_rw)
        last_rw[0] = tok_rw

    for h in range(2):
        t0 = h * HT
        flush_pend()
        if last_rw[0] is not None:
            E["gp"].wait(last_rw[0])
        tok_mix = None
        tok_mix_m = {}
        for m in ("k", "v", "r"):
            for g in range(NG):
                E["gp"].quiet(lambda e, g=g, t0=t0: e.tensor_sub(
                    TmixT, xnt_g(g)[:, 1 + t0:1 + t0 + HT],
                    xnt_g(g)[:, t0:t0 + HT]))
                E["gp"].quiet(lambda e, m=m, g=g: e.tensor_mul(
                    TmixU, TmixT, bcast(TM[m][:, g:g + 1], HT)))
                tok_mix = E["gp"].op(
                    lambda e, m=m, g=g, t0=t0: e.tensor_add(
                        XM[m][:, g * HT:(g + 1) * HT], TmixU,
                        xnt_g(g)[:, t0:t0 + HT]))
            tok_mix_m[m] = tok_mix

        def mm_proj(wt_ap, mname, c, bank, g):
            E["te"].wait(tok_mix_m[mname])
            tok = None
            for dg in range(NG):
                tok = E["te"].op(
                    lambda e, dg=dg, c=c, bank=bank, g=g, wt_ap=wt_ap,
                    mname=mname: e.matmul(
                        pa(bank),
                        wt_ap[:, dg * 1024 + g * 128:dg * 1024 + (g + 1) * 128],
                        XM[mname][:, dg * HT + c * 512:dg * HT + (c + 1) * 512],
                        start=(dg == 0), stop=(dg == NG - 1)))
            return tok

        for g in range(NG):
            flush_pend()
            gb = 4 * (g % 2)
            tok_ek = None
            for c in range(2):
                bank = gb + c
                E["te"].wait(ev_guard.get(bank))
                tokk = mm_proj(WT["wk"], "k", c, bank, g)
                E["s"].wait(tokk)
                tok_ek = E["s"].op(lambda e, c=c, bank=bank: e.activation(
                    Tek[:, c * 512:(c + 1) * 512], pa(bank), AF.Exp,
                    bias=0.0, scale=1.0))
                ev_guard[bank] = tok_ek
            tok_v = None
            for c in range(2):
                bank = gb + 2 + c
                E["te"].wait(ev_guard.get(bank))
                tokv = mm_proj(WT["wv"], "v", c, bank, g)
                E["s"].wait(tokv)
                tok_v = E["s"].op(lambda e, c=c, bank=bank: e.activation(
                    Tv[:, c * 512:(c + 1) * 512], pa(bank), AF.Copy,
                    bias=0.0, scale=1.0))
                ev_guard[bank] = tok_v
            tok_sr = None
            for c in range(2):
                bank = gb + c
                E["te"].wait(ev_guard.get(bank))
                tokr = mm_proj(WT["wr"], "r", c, bank, g)
                E["s"].wait(tokr)
                tok_sr = E["s"].op(lambda e, c=c, bank=bank: e.activation(
                    Tsd[:, c * 512:(c + 1) * 512], pa(bank), AF.Sigmoid,
                    bias=0.0, scale=1.0))
                ev_guard[bank] = tok_sr

            eug = eu8[:, g:g + 1]
            ewb = bcast(ew8[:, g:g + 1], HT)
            if g == 0:
                E["v"].wait(tok_mix_m["r"])
            E["v"].waits([tok_ek, tok_v])
            E["v"].quiet(lambda e: e.tensor_mul(Tz, Tek, Tv))
            if h == 0:
                E["v"].quiet(lambda e, ewb=ewb: e.tensor_tensor_scan(
                    Ta, ewb, Tz, 0.0, ALU.mult, ALU.add))
                E["v"].quiet(lambda e, ewb=ewb: e.tensor_tensor_scan(
                    Tb, ewb, Tek, 0.0, ALU.mult, ALU.add))
                E["v"].quiet(lambda e, eug=eug: e.tensor_scalar_mul(
                    Tv[:, 0:1], Tz[:, 0:1], eug))
            else:
                E["v"].quiet(lambda e, ewb=ewb, g=g: e.tensor_tensor_scan(
                    Ta, ewb, Tz, carryA[:, g:g + 1], ALU.mult, ALU.add))
                E["v"].quiet(lambda e, ewb=ewb, g=g: e.tensor_tensor_scan(
                    Tb, ewb, Tek, carryB[:, g:g + 1], ALU.mult, ALU.add))
                E["v"].quiet(lambda e, eug=eug, g=g: e.scalar_tensor_tensor(
                    Tv[:, 0:1], Tz[:, 0:1], eug, carryA[:, g:g + 1],
                    ALU.mult, ALU.add))
            E["v"].quiet(lambda e, eug=eug: e.scalar_tensor_tensor(
                Tv[:, 1:HT], Tz[:, 1:HT], eug, Ta[:, 0:HT - 1],
                ALU.mult, ALU.add))
            if h == 0:
                E["v"].quiet(lambda e, eug=eug: e.tensor_scalar_mul(
                    Td[:, 0:1], Tek[:, 0:1], eug))
            else:
                E["v"].quiet(lambda e, eug=eug, g=g: e.scalar_tensor_tensor(
                    Td[:, 0:1], Tek[:, 0:1], eug, carryB[:, g:g + 1],
                    ALU.mult, ALU.add))
            tok_den = E["v"].op(lambda e, eug=eug: e.scalar_tensor_tensor(
                Td[:, 1:HT], Tek[:, 1:HT], eug, Tb[:, 0:HT - 1],
                ALU.mult, ALU.add))
            E["v"].quiet(lambda e, g=g: e.tensor_copy(carryA[:, g:g + 1],
                                                      Ta[:, HT - 1:HT]))
            tok_carry = E["v"].op(lambda e, g=g: e.tensor_copy(
                carryB[:, g:g + 1], Tb[:, HT - 1:HT]))
            E["s"].waits([tok_den, tok_carry])
            E["s"].quiet(lambda e: e.activation(Tek, Td, AF.Ln, bias=0.0,
                                                scale=1.0))
            tok_rd = E["s"].op(lambda e: e.activation(Td, Tek, AF.Exp,
                                                      bias=0.0, scale=-1.0))
            pend[0] = (tok_rd, tok_sr, g, t0)

    flush_pend()
    E["v"].self_sync = True
    barrier()

    tok_ones1 = E["v"].op(lambda e: e.memset(ones128f, 1.0))
    E["v"].wait(tok_ones)

    def prep_gu(src_h, dst_s, mwb_dst, q, bank, pbase, acol, veng, e4off,
                ab_toks):
        NB = 4
        wrow = [reg(pbase + i * 4 * KB, 4 * KB, F32) for i in range(NB)]
        qi = reg(pbase + NB * 4 * KB, 4 * KB, I32)
        tern = reg(pbase + NB * 4 * KB + 4 * KB, 2 * KB, BF16)
        est4 = reg(e4off, 8 * KB, BF16)
        nrc = FI // 128
        E["v"].wait(ab_toks[-1])
        tok_as = E["v"].gop(lambda e: e.reduce_sum(
            asum_v[:, :], asum[:, acol:acol + 32], axis=AX.X))
        E["te"].waits([tok_as, tok_ones1, ev_guard.get(bank)])
        tok_ts = E["te"].op(lambda e, bank=bank: e.matmul(
            ps[bank][:, 0:1], ones128f, asum_v, start=True, stop=True))
        E["s"].wait(tok_ts)
        tok_mw = E["s"].op(lambda e, bank=bank: e.activation(
            mwb_dst, ps[bank][:, 0:1], AF.Copy, bias=0.0,
            scale=1.0 / (FI * D)))
        ev_guard[bank] = tok_mw
        E["v"].wait(tok_mw)
        swb_m = sm(372 + (0 if q == "sync" else 1), 1)
        E["v"].quiet(lambda e, swb_m=swb_m: e.tensor_scalar_max(
            swb_m, mwb_dst, EPS))
        tok_bc2 = E["v"].gop(lambda e, swb_m=swb_m: e.reciprocal(swb_m, swb_m))
        q_toks = [None] * nrc
        E[q].wait(ab_toks[-1])
        prev_store = [None, None]
        for rc in range(nrc):
            if rc >= NB:
                E[q].wait(q_toks[rc - NB])
            tok_ld = dma(q, wrow[rc % NB], src_h[rc * 128:(rc + 1) * 128, :])
            if veng == "s":
                E["s"].waits([tok_ld, tok_bc2])
                tok_q = E["s"].op(lambda e, rc=rc: e.activation(
                    qi, wrow[rc % NB], AF.Copy, bias=0.0, scale=swb_m))
            else:
                E["v"].waits([tok_ld, tok_bc2])
                tok_q = E["v"].op(lambda e, rc=rc: e.tensor_scalar(
                    qi, wrow[rc % NB], swb_m, None, ALU.mult))
            q_toks[rc] = tok_q
            E["v"].wait(tok_q)
            tok_t = E["v"].op(lambda e: e.tensor_scalar(
                tern, qi, 1.0, -1.0, ALU.min, ALU.max))
            E["s"].wait(tok_t) if veng == "s" else None
            rcs = rc % 4
            for half in range(2):
                bk = bank if half else bank - 2
                E["te"].waits([tok_t, ev_guard.get(bk)])
                tok_mm = None
                for r in range(4):
                    dg = half * 4 + r
                    tok_mm = E["te"].op(
                        lambda e, dg=dg, r=r, bk=bk: e.matmul(
                            pa(bk, r * 128, (r + 1) * 128),
                            tern[:, dg * 128:(dg + 1) * 128], ident,
                            start=True, stop=True))
                v4 = est4.rearrange("p (dg rr c) -> p dg rr c", rr=4, c=128)
                dstap = v4[:, half * 4:(half + 1) * 4, rcs:rcs + 1, :]
                E[veng].waits([tok_mm, prev_store[0]])
                if veng == "s":
                    tok_ev = E[veng].op(lambda e, d=dstap, bk=bk:
                                        e.activation(d, pa(bk), AF.Copy,
                                                     bias=0.0, scale=1.0))
                else:
                    tok_ev = E[veng].op(lambda e, d=dstap, bk=bk:
                                        e.tensor_copy(d, pa(bk)))
                ev_guard[bk] = tok_ev
            if rcs == 3:
                E["s"].wait(tok_ev)
                prev_store[0] = dma("s", dst_s[rc // 4], est4)
            yield

    def prep_downA():
        wrow_d = reg(R_WT + 8 * KB, 16 * KB, F32)
        absout_d = reg(R_WT + 24 * KB, 16 * KB, F32)
        toks = [None] * 8
        for rc in range(8):
            q = ("sync", "gp")[rc % 2]
            if rc >= 1:
                E[q].wait(toks[rc - 1])
            tok_ld = dma(q, wrow_d, w_h["wd"][rc * 128:(rc + 1) * 128, :])
            E["s"].wait(tok_ld)
            toks[rc] = E["s"].op(lambda e, rc=rc: e.activation(
                absout_d, wrow_d, AF.Abs, bias=0.0, scale=1.0,
                accum_out=asum_d[:, rc:rc + 1]))
            for _ in range(3):
                yield toks

    def prep_down(ab_toks):
        wrow = [reg(POOL, 16 * KB, F32), reg(POOL + 16 * KB, 16 * KB, F32)]
        est4 = reg(R_WT, 32 * KB, BF16)
        qi = reg(R_WT + 32 * KB, 16 * KB, I32)
        absout = reg(R_WT + 32 * KB, 16 * KB, F32)
        tern = reg(POOL + 32 * KB, 8 * KB, BF16)
        nrc = 8
        qs = ["sync", "gp"]
        E["v"].wait(ab_toks[-1])
        tok_as = E["v"].gop(lambda e: e.reduce_sum(
            asum_v[:, :], asum_d[:, 0:nrc], axis=AX.X))
        E["te"].waits([tok_as, tok_ones1, ev_guard.get(6)])
        tok_ts = E["te"].op(lambda e: e.matmul(
            ps[6][:, 0:1], ones128f, asum_v, start=True, stop=True))
        E["s"].wait(tok_ts)
        tok_mw = E["s"].op(lambda e: e.activation(
            mwb["wd"], ps[6][:, 0:1], AF.Copy, bias=0.0, scale=1.0 / (FI * D)))
        ev_guard[6] = tok_mw
        E["v"].wait(tok_mw)
        E["v"].quiet(lambda e: e.tensor_scalar_max(swb, mwb["wd"], EPS))
        tok_bc2 = E["v"].gop(lambda e: e.reciprocal(swb, swb))
        q_toks = [None] * nrc
        E["sync"].wait(ab_toks[-1])
        E["gp"].wait(ab_toks[-1])
        prev_store = [None, None, None, None]
        for rc in range(nrc):
            q = qs[rc % 2]
            if rc >= 2:
                E[q].wait(q_toks[rc - 2])
            tok_ld = dma(q, wrow[rc % 2], w_h["wd"][rc * 128:(rc + 1) * 128, :])
            E["s"].waits([tok_ld, tok_bc2])
            tok_q = E["s"].op(lambda e, rc=rc: e.activation(
                qi, wrow[rc % 2], AF.Copy, bias=0.0, scale=swb))
            q_toks[rc] = tok_q
            E["v"].wait(tok_q)
            tok_t = E["v"].op(lambda e: e.tensor_scalar(
                tern, qi, 1.0, -1.0, ALU.min, ALU.max))
            E["s"].wait(tok_t)
            rcs = rc % 4
            for half in range(8):
                bank = 6 + half % 2
                E["te"].waits([tok_t, ev_guard.get(bank)])
                tok_mm = None
                for r in range(4):
                    ig = half * 4 + r
                    tok_mm = E["te"].op(
                        lambda e, ig=ig, r=r, bank=bank: e.matmul(
                            pa(bank, r * 128, (r + 1) * 128),
                            tern[:, ig * 128:(ig + 1) * 128], ident,
                            start=True, stop=True))
                v4 = est4.rearrange("p (ig rr c) -> p ig rr c", rr=4, c=128)
                dstap = v4[:, half * 4:(half + 1) * 4, rcs:rcs + 1, :]
                E["v"].waits([tok_mm] + prev_store)
                tok_ev = E["v"].op(lambda e, d=dstap, bank=bank:
                                   e.tensor_copy(d, pa(bank)))
                ev_guard[bank] = tok_ev
            if rcs == 3:
                eg = rc // 4
                E["s"].wait(tok_ev)
                for part in range(4):
                    dst = wdT_s[part * 8:(part + 1) * 8, eg, :, :]
                    dst = dst.transpose([1, 0, 2])
                    srcap = est4[:, part * 8 * 512:(part + 1) * 8 * 512]
                    prev_store[part] = dma("s", dst, srcap)

    g1 = prep_gu(w_h["wg"], wgT_s, mwb["wg"], "sync", 6, POOL, 0, "v",
                 POOL + 22 * KB, earlyAB["wg"])
    g2 = prep_gu(w_h["wu"], wuT_s, mwb["wu"], "gp", 7, POOL + 30 * KB, 32, "s",
                 R_WT, earlyAB["wu"])
    g3 = prep_downA()
    dn_toks = [None]
    alive = [g1, g2, g3]
    while alive:
        for gg in list(alive):
            try:
                r = next(gg)
                if gg is g3:
                    dn_toks[0] = r
            except StopIteration:
                alive.remove(gg)
    barrier()
    prep_down(dn_toks[0])
    barrier()
    E["v"].quiet(lambda e: e.tensor_scalar(isg, inv_s, mwb["wg"], None,
                                           ALU.mult))
    tok_is = E["v"].op(lambda e: e.tensor_scalar(isu, inv_s, mwb["wu"], None,
                                                 ALU.mult))
    E["s"].wait(tok_is)

    qxiT = reg(R_XNT, 32 * KB, BF16)
    tok_qxiT = dma("sync", qxiT.rearrange("p (g t) -> p g t", g=NG),
                   qxT_s.transpose([1, 0, 2]))
    E["te"].wait(tok_qxiT)
    barrier()

    wgc = [reg(R_WT, 8 * KB, BF16), reg(R_WT + 8 * KB, 8 * KB, BF16)]
    wuc = [reg(R_WT + 16 * KB, 8 * KB, BF16), reg(R_WT + 24 * KB, 8 * KB, BF16)]
    qhTp = [reg(R_WT + 32 * KB, 16 * KB, BF16),
            reg(POOL + 36 * KB, 16 * KB, BF16)]
    hbuf = reg(POOL, 32 * KB, BF16)
    hi8 = reg(POOL + 32 * KB, 4 * KB, I8)
    fin = [reg(VROW, 2 * KB, F32), reg(VROW + 2 * KB, 2 * KB, F32)]
    xld = [reg(R_QX8, 4 * KB, F32), reg(R_QX8 + 4 * KB, 4 * KB, F32)]
    wdc = [reg(R_QX8 + 8 * KB, 4 * KB, BF16), reg(R_QX8 + 12 * KB, 4 * KB, BF16)]
    f1s = reg(POOL + 52 * KB, 2 * KB, F32)
    dsc = [reg(768 + 2688 + i * 256, 256, BF16) for i in range(4)]
    rgs = reg(VROW, 2 * KB, F32)
    rus = reg(VROW + 2 * KB, 2 * KB, F32)
    sqs = f1s

    n_store = [0]
    store_toks = []
    prev_fin = [None]
    prev_hq_done = [None]
    prev_finV = [None]

    for q4 in range(4):
        tts = [q4 * 4 + i for i in range(4)]
        tq0 = q4 * 512
        hb0_tok = [None]
        hb_toks = {}
        dsc_toks = {}
        E["s"].wait(prev_fin[0])
        E["s"].waits(store_toks[-2:])
        E["v"].wait(prev_hq_done[0])
        tok_wg = {}
        tok_wu = {}
        last_te_wg = {0: ev_guard.get(("wgc", 0)), 1: ev_guard.get(("wgc", 1))}
        tok_h = None
        for ic in range(8):
            bi = ic % 2
            E["sync"].wait(last_te_wg[bi])
            tok_wg[ic] = dma("sync", wgc[bi], wgT_s[ic])
            tok_wu[ic] = dma("sync", wuc[bi], wuT_s[ic])
            for jq in range(4):
                bankg, banku = (jq % 2) * 2, (jq % 2) * 2 + 1
                E["te"].waits([tok_wg[ic], tok_wu[ic],
                               ev_guard.get(bankg), ev_guard.get(banku)])
                tok_g = None
                tok_u = None
                for dg in range(NG):
                    lhs = qxiT[:, dg * T + tq0 + jq * 128:
                               dg * T + tq0 + (jq + 1) * 128]
                    tok_g = E["te"].op(
                        lambda e, lhs=lhs, dg=dg, bi=bi, bankg=bankg:
                        e.matmul(pa(bankg), lhs,
                                 wgc[bi][:, dg * 512:(dg + 1) * 512],
                                 start=(dg == 0), stop=(dg == NG - 1)))
                for dg in range(NG):
                    lhs = qxiT[:, dg * T + tq0 + jq * 128:
                               dg * T + tq0 + (jq + 1) * 128]
                    tok_u = E["te"].op(
                        lambda e, lhs=lhs, dg=dg, bi=bi, banku=banku:
                        e.matmul(pa(banku), lhs,
                                 wuc[bi][:, dg * 512:(dg + 1) * 512],
                                 start=(dg == 0), stop=(dg == NG - 1)))
                last_te_wg[bi] = tok_u
                tcol = tts[jq]
                E["s"].waits([tok_g, tok_u])
                tok_rg = E["s"].op(lambda e, bankg=bankg, tcol=tcol:
                                   e.activation(rgs, pa(bankg), AF.Relu,
                                                bias=0.0,
                                                scale=isg[:, tcol:tcol + 1]))
                ev_guard[bankg] = tok_rg
                tok_ru = E["s"].op(lambda e, banku=banku, tcol=tcol:
                                   e.activation(rus, pa(banku), AF.Copy,
                                                bias=0.0,
                                                scale=isu[:, tcol:tcol + 1]))
                ev_guard[banku] = tok_ru
                tok_sq = E["s"].op(lambda e: e.activation(sqs, rgs, AF.Square,
                                                          bias=0.0, scale=1.0))
                E["v"].waits([tok_sq, tok_ru])
                tok_h = E["v"].op(lambda e, jq=jq, ic=ic: e.tensor_mul(
                    hbuf[:, jq * 4096 + ic * 512:jq * 4096 + (ic + 1) * 512],
                    sqs, rus))
                E["v"].gop(lambda e, jq=jq, ic=ic: e.reduce_max(
                    mhp[jq][:, ic:ic + 1],
                    hbuf[:, jq * 4096 + ic * 512:jq * 4096 + (ic + 1) * 512],
                    axis=AX.X, apply_absolute_value=True))
                E["s"].wait(tok_h)
                if ic == 7 and jq == 0:
                    tt = tts[0]
                    hj = hbuf[:, 0:4096]
                    E["v"].gop(lambda e, tt=tt: e.reduce_max(
                        mh_t[:, tt:tt + 1], mhp[0], axis=AX.X))
                    E["v"].quiet(lambda e, tt=tt: e.tensor_scalar_max(
                        tq2[:, tt:tt + 1], mh_t[:, tt:tt + 1], EPS))
                    E["v"].gop(lambda e, tt=tt: e.reciprocal(
                        sh_t[:, tt:tt + 1], tq2[:, tt:tt + 1]))
                    E["v"].quiet(lambda e, tt=tt: e.tensor_scalar_mul(
                        sh_t[:, tt:tt + 1], sh_t[:, tt:tt + 1], 127.0))
                    E["v"].quiet(lambda e, tt=tt: e.tensor_scalar(
                        scd[:, tt:tt + 1], tq2[:, tt:tt + 1], mwb["wd"],
                        None, ALU.mult))
                    tok_sc0 = E["v"].op(lambda e, tt=tt: e.tensor_scalar_mul(
                        scd[:, tt:tt + 1], scd[:, tt:tt + 1], 1.0 / 127.0))
                    E["s"].wait(tok_sc0)
                    tok_hq0 = E["s"].op(lambda e, hj=hj, tt=tt: e.activation(
                        hi8, hj, AF.Copy, bias=0.0,
                        scale=sh_t[:, tt:tt + 1]))
                    hb0_tok[0] = E["s"].op(lambda e: e.activation(
                        hbuf[:, 0:4096], hi8, AF.Copy, bias=0.0, scale=1.0))
        ev_guard[("wgc", 0)] = last_te_wg[0]
        ev_guard[("wgc", 1)] = last_te_wg[1]

        for jq in range(4):
            tt = tts[jq]
            hj = hbuf[:, jq * 4096:(jq + 1) * 4096]
            if jq == 0:
                hb_toks[0] = hb0_tok[0]
                tok_d0 = E["v"].op(lambda e, tt=tt: e.tensor_scalar(
                    dsc[0], ident, scd[:, tt:tt + 1], None, ALU.mult))
                dsc_toks[0] = tok_d0
                continue
            E["v"].gop(lambda e, jq=jq, tt=tt: e.reduce_max(
                mh_t[:, tt:tt + 1], mhp[jq], axis=AX.X))
            E["v"].quiet(lambda e, tt=tt: e.tensor_scalar_max(
                tq2[:, tt:tt + 1], mh_t[:, tt:tt + 1], EPS))
            E["v"].gop(lambda e, tt=tt: e.reciprocal(
                sh_t[:, tt:tt + 1], tq2[:, tt:tt + 1]))
            E["v"].quiet(lambda e, tt=tt: e.tensor_scalar_mul(
                sh_t[:, tt:tt + 1], sh_t[:, tt:tt + 1], 127.0))
            E["v"].quiet(lambda e, tt=tt: e.tensor_scalar(
                scd[:, tt:tt + 1], tq2[:, tt:tt + 1], mwb["wd"], None,
                ALU.mult))
            tok_sc = E["v"].op(lambda e, tt=tt: e.tensor_scalar_mul(
                scd[:, tt:tt + 1], scd[:, tt:tt + 1], 1.0 / 127.0))
            dsc_toks[jq] = E["v"].op(lambda e, jq=jq, tt=tt: e.tensor_scalar(
                dsc[jq], ident, scd[:, tt:tt + 1], None, ALU.mult))
            E["s"].wait(tok_sc)
            tok_hq = E["s"].op(lambda e, hj=hj, tt=tt: e.activation(
                hi8, hj, AF.Copy, bias=0.0, scale=sh_t[:, tt:tt + 1]))
            qhbf = hbuf[:, jq * 4096:(jq + 1) * 4096]
            hb_toks[jq] = E["s"].op(lambda e, qhbf=qhbf: e.activation(
                qhbf, hi8, AF.Copy, bias=0.0, scale=1.0))

        E["sync"].wait(prev_finV[0])
        xt_toks = [dma("sync", xld[j], x_h[tts[j] * 128:(tts[j] + 1) * 128, :])
                   for j in range(2)]
        xt_toks2 = None

        def att_mms(o):
            pr, j2, ec = o >> 2, (o >> 1) & 1, o & 1
            jq = pr * 2 + j2
            last = None
            for g in range(NG):
                last = E["te"].op(
                    lambda e, g=g, o=o, ec=ec, tq0=tq0, jq=jq:
                    e.matmul(pa(o),
                             rwkvT[:, g * T + tq0 + jq * 128:
                                   g * T + tq0 + (jq + 1) * 128],
                             WoT[:, g * 1024 + ec * 512:
                                 g * 1024 + (ec + 1) * 512],
                             start=(g == 0), stop=False))
            return last

        for o in range(6):
            E["te"].wait(ev_guard.get(o))
            att_mms(o)

        tok_tr_done = None
        for jq in range(4):
            pr, j2 = jq // 2, jq % 2
            qhbf = hbuf[:, jq * 4096:(jq + 1) * 4096]
            E["te"].waits([hb_toks[jq], dsc_toks[jq]])
            for half in range(8):
                bank = 6 + half % 2
                srcs = [qhbf[:, (half * 4 + r) * 128:
                             (half * 4 + r + 1) * 128] for r in range(4)]
                tok_tr_done = trans_batch(
                    srcs, qhTp[pr][:, j2 * 4096 + half * 512:
                                   j2 * 4096 + (half + 1) * 512], bank,
                    eng=E["v"], rhs=dsc[jq])
        prev_hq_done[0] = tok_tr_done

        for o in (6, 7):
            E["te"].wait(ev_guard.get(o))
            att_mms(o)

        tok_dn = {}
        for igp in range(NIG // 2):
            bi = igp % 2
            E["sync"].wait(ev_guard.get(("wdc", bi)))
            tok_wd = dma("sync",
                         wdc[bi].rearrange("p (i g c) -> p i g c", i=2, g=2),
                         wdT_s[igp * 2:igp * 2 + 2].transpose([2, 0, 1, 3]))
            E["te"].waits([tok_wd, tok_tr_done])
            last = None
            for isub in range(2):
                ig = igp * 2 + isub
                for o in range(8):
                    pr, j2, ec = o >> 2, (o >> 1) & 1, o & 1
                    last = E["te"].op(
                        lambda e, pr=pr, j2=j2, ec=ec, ig=ig, bi=bi, o=o,
                        isub=isub:
                        e.matmul(pa(o),
                                 qhTp[pr][:, j2 * 4096 + ig * 128:
                                          j2 * 4096 + (ig + 1) * 128],
                                 wdc[bi][:, isub * 1024 + ec * 512:
                                         isub * 1024 + (ec + 1) * 512],
                                 start=False, stop=(ig == NIG - 1)))
                    if ig == NIG - 1:
                        tok_dn[o] = last
            ev_guard[("wdc", bi)] = last

        for o in range(8):
            pr, j2, ec = o >> 2, (o >> 1) & 1, o & 1
            jq = pr * 2 + j2
            tcol = tts[jq]
            if o == 4:
                E["sync"].wait(prev_fin[0])
                xt_toks2 = [dma("sync", xld[j],
                                x_h[tts[2 + j] * 128:(tts[2 + j] + 1) * 128, :])
                            for j in range(2)]
            E["v"].wait(tok_dn[o])
            E["v"].waits(xt_toks if pr == 0 else xt_toks2)
            if len(store_toks) >= 2:
                E["v"].wait(store_toks[-2])
            tok_fin = E["v"].op(lambda e, o=o, j2=j2, ec=ec: e.tensor_tensor(
                fin[ec], pa(o), xld[j2][:, ec * 512:(ec + 1) * 512], ALU.add))
            ev_guard[o] = tok_fin
            prev_fin[0] = tok_fin
            prev_finV[0] = tok_fin
            E["gp"].wait(tok_fin)
            st_tok = dma("gp", out_h[tcol * 128:(tcol + 1) * 128,
                                     ec * 512:(ec + 1) * 512], fin[ec])
            store_toks.append(st_tok)
            n_store[0] += 1

    for s in QPOOL["gp"]:
        E["gp"].wait((s, s.n))

    with nc.Block() as block:
        @block.sync
        def _(e):
            for f in E["sync"].fns:
                f(e)

        @block.gpsimd
        def _(e):
            for f in E["gp"].fns:
                f(e)

        @block.tensor
        def _(e):
            for f in E["te"].fns:
                f(e)

        @block.vector
        def _(e):
            for f in E["v"].fns:
                f(e)

        @block.scalar
        def _(e):
            for f in E["s"].fns:
                f(e)

    return nc


_NC_CACHE = []


def kernel(L0_attn_x, L0_attn_time_decay, L0_attn_time_first,
           L0_attn_time_mix_k, L0_attn_time_mix_v, L0_attn_time_mix_r,
           L0_attn_Wk, L0_attn_Wv, L0_attn_Wr, L0_attn_Wo,
           L0_ffn_w_gate, L0_ffn_w_up, L0_ffn_w_down):
    x = np.ascontiguousarray(np.asarray(L0_attn_x, dtype=np.float32))
    B = x.shape[0]
    com = {
        "wk": np.ascontiguousarray(np.asarray(L0_attn_Wk, np.float32)),
        "wv": np.ascontiguousarray(np.asarray(L0_attn_Wv, np.float32)),
        "wr": np.ascontiguousarray(np.asarray(L0_attn_Wr, np.float32)),
        "wo": np.ascontiguousarray(np.asarray(L0_attn_Wo, np.float32)),
        "wg": np.ascontiguousarray(np.asarray(L0_ffn_w_gate, np.float32)),
        "wu": np.ascontiguousarray(np.asarray(L0_ffn_w_up, np.float32)),
        "wd": np.ascontiguousarray(np.asarray(L0_ffn_w_down, np.float32)),
        "td": np.ascontiguousarray(
            np.asarray(L0_attn_time_decay, np.float32).reshape(1, D)),
        "tf": np.ascontiguousarray(
            np.asarray(L0_attn_time_first, np.float32).reshape(1, D)),
        "tmk": np.ascontiguousarray(
            np.asarray(L0_attn_time_mix_k, np.float32).reshape(1, D)),
        "tmv": np.ascontiguousarray(
            np.asarray(L0_attn_time_mix_v, np.float32).reshape(1, D)),
        "tmr": np.ascontiguousarray(
            np.asarray(L0_attn_time_mix_r, np.float32).reshape(1, D)),
    }
    if not _NC_CACHE:
        _NC_CACHE.append(build_nc())
    nc = _NC_CACHE[0]
    in_maps = [dict(com, x=np.ascontiguousarray(x[c])) for c in range(B)]
    res = run_bass_kernel_spmd(nc, in_maps, core_ids=list(range(B)))
    return np.stack([np.asarray(r["out"], np.float32) for r in res.results])


# version 34
# speedup vs baseline: 1.0989x; 1.0989x over previous
import numpy as np
import ml_dtypes
import concourse.bass as bass
import concourse.mybir as mybir
from concourse.bass_utils import run_bass_kernel_spmd

F32 = mybir.dt.float32
BF16 = mybir.dt.bfloat16
I32 = mybir.dt.int32
I8 = mybir.dt.int8
AF = mybir.ActivationFunctionType
ALU = mybir.AluOpType
AX = mybir.AxisListType

T = 2048
D = 1024
FI = 4096
NTT = T // 128
NG = D // 128
NIG = FI // 128
EPS = 1e-5
KB = 1024
HT = 1024
NXT = T + 1


class Sem:
    __slots__ = ("h", "n")

    def __init__(self, h):
        self.h = h
        self.n = 0


class Eng:
    def __init__(self, sem, self_sync=False):
        self.sem = sem
        self.self_sync = self_sync
        self.fns = []
        self.waited = {}

    def wait(self, tok):
        if tok is None:
            return
        sem, val = tok
        if val <= 0 or self.waited.get(id(sem), 0) >= val:
            return
        self.waited[id(sem)] = val
        self.fns.append(lambda e, s=sem, v=val: e.wait_ge(s.h, v))

    def waits(self, toks):
        for t in toks:
            self.wait(t)

    def op(self, fn, inc=1):
        sem = self.sem
        sem.n += inc

        def run(e, f=fn, s=sem, i=inc):
            f(e).then_inc(s.h, i)
        self.fns.append(run)
        tok = (sem, sem.n)
        if self.self_sync:
            self.wait(tok)
        return tok

    def quiet(self, fn):
        if self.self_sync:
            return self.op(lambda e, f=fn: f(e))
        self.fns.append(fn)
        return None

    def gop(self, fn):
        self.wait((self.sem, self.sem.n))
        sem = self.sem
        sem.n += 1

        def run(e, f=fn, s=sem):
            f(e).then_inc(s.h, 1)
        self.fns.append(run)
        tok = (sem, sem.n)
        self.wait(tok)
        return tok


def build_nc(debug=False):
    nc = bass.Bass("TRN2", target_bir_lowering=False,
                   detect_race_conditions=debug)

    x_h = nc.dram_tensor("x", [T, D], F32, kind="ExternalInput")[:]
    out_h = nc.dram_tensor("out", [T, D], F32, kind="ExternalOutput")[:]
    w_h = {}
    for nm, shp in (("wk", [D, D]), ("wv", [D, D]), ("wr", [D, D]),
                    ("wo", [D, D]), ("wg", [FI, D]), ("wu", [FI, D]),
                    ("wd", [D, FI])):
        w_h[nm] = nc.dram_tensor(nm, shp, F32, kind="ExternalInput")[:]
    vec_h = {}
    for nm in ("td", "tf", "tmk", "tmv", "tmr"):
        vec_h[nm] = nc.dram_tensor(nm, [1, D], F32, kind="ExternalInput")[:]
    wgT_s = nc.dram_tensor("wgT_s", [8, 128, FI], BF16, kind="Internal")[:]
    wuT_s = nc.dram_tensor("wuT_s", [8, 128, FI], BF16, kind="Internal")[:]
    wdT_s = nc.dram_tensor("wdT_s", [NIG, 2, 128, 512], BF16, kind="Internal")[:]
    qxT_s = nc.dram_tensor("qxT_s", [NG, 128, T], BF16, kind="Internal")[:]

    ARENA_B = 212800
    BASE = (nc.sbuf_base + 63) & ~63
    esz = {F32: 4, BF16: 2, I32: 4, I8: 1}
    _allocs = {}

    def reg(off_b, nbytes, dt):
        assert off_b % 4 == 0 and off_b + nbytes <= ARENA_B, (off_b, nbytes)
        adt = dt if dt in (F32, BF16) else F32
        key = (off_b, nbytes, adt)
        if key not in _allocs:
            shape = [128, nbytes // esz[adt]]
            _allocs[key] = nc.alloc_sbuf_tensor_at(
                f"rg{off_b}_{nbytes}_{adt}", shape, adt, offset=BASE + off_b)
        h = _allocs[key]
        if dt == adt:
            return h[:, 0:nbytes // esz[dt]]
        return h.bitcast(dt)[:, 0:nbytes // esz[dt]]

    ident = reg(0, 256, BF16)
    ii32 = reg(256, 512, I32)
    ones128f = reg(256, 512, F32)
    smalls = reg(768, 4 * KB, F32)

    def sm(c, n):
        return smalls[:, c:c + n]
    S1, S2 = sm(0, 16), sm(16, 16)
    mtile, ex2 = sm(32, 16), sm(48, 16)
    rstile, nmrs = sm(64, 16), sm(80, 16)
    mxt, tqt = sm(96, 16), sm(112, 16)
    sxt, inv_s = sm(128, 16), sm(144, 16)
    isg, isu = sm(160, 16), sm(176, 16)
    mh_t, tq2 = sm(192, 16), sm(208, 16)
    sh_t, scd = sm(224, 16), sm(240, 16)
    asum = sm(384, 64)
    carryA, carryB = sm(288, 8), sm(296, 8)
    td8, tf8 = sm(304, 8), sm(312, 8)
    ew8, eu8 = sm(320, 8), sm(328, 8)
    tmk8, tmv8, tmr8 = sm(336, 8), sm(344, 8), sm(352, 8)
    ones_col = sm(360, 1)
    mwb = {"wg": sm(361, 1), "wu": sm(362, 1), "wd": sm(363, 1)}
    swb = sm(364, 1)
    mw11 = sm(368, 1)
    rsw11 = sm(369, 1)
    asum_v = sm(370, 1)
    epsb = sm(371, 1)
    mhp = [sm(448, 8), sm(456, 8), sm(464, 8), sm(472, 8)]
    asum_d = sm(480, 8)

    VROW = 4864
    R_XNT = 8960
    R_QX8 = 41856
    R_RWKV = 58240
    R_WT = 91008
    POOL = 156544

    ps = [nc.alloc_psum_tensor(f"psb{i}", [128, 512], F32) for i in range(8)]

    def pa(i, a=0, b=512):
        return ps[i][:, a:b]

    import contextlib
    stack = contextlib.ExitStack()
    nc._keepalive_stack = stack
    sems = {nm: Sem(stack.enter_context(nc.semaphore(nm)))
            for nm in ("gc", "te", "vv", "ss")}
    E = {"sync": Eng(None), "gp": Eng(sems["gc"], self_sync=debug),
         "te": Eng(sems["te"]),
         "v": Eng(sems["vv"], self_sync=True),
         "s": Eng(sems["ss"], self_sync=debug)}
    NQS = 6
    QPOOL = {q: [Sem(stack.enter_context(nc.semaphore(f"{q}{i}")))
                 for i in range(NQS)] for q in ("sync", "gp", "s")}
    QRR = {q: 0 for q in QPOOL}

    def dma(q, out_ap, in_ap, slow=False):
        pool = QPOOL[q]
        s = pool[QRR[q]]
        QRR[q] = (QRR[q] + 1) % NQS
        if s.n > 0:
            E[q].wait((s, s.n))
        s.n += 16

        def run(en, o=out_ap, i=in_ap, ss=s, sl=slow):
            en.dma_start(out=o, in_=i,
                         allow_slow_non_contiguous=sl).then_inc(ss.h, 16)
        E[q].fns.append(run)
        return (s, s.n)

    def barrier():
        toks = [(E[k].sem, E[k].sem.n) for k in E if E[k].sem is not None]
        toks += [(s, s.n) for q in QPOOL for s in QPOOL[q]]
        for k in E:
            E[k].waits(toks)

    def bcast(ap1, n):
        return ap1.broadcast_to((128, n))

    ev_guard = {}

    t_iota = E["gp"].op(lambda e: e.iota(ii32, pattern=[[1, 128]], base=0,
                                         channel_multiplier=-1))
    E["v"].wait(t_iota)
    tok_ident = E["v"].op(lambda e: e.tensor_scalar(ident, ii32, 0.0, None,
                                                    ALU.is_equal))
    tok_ones = E["v"].op(lambda e: e.memset(ones_col, 1.0))
    tok_eps = E["v"].op(lambda e: e.memset(epsb, EPS))

    xf32 = reg(R_WT, 64 * KB, F32)
    tok_x = dma("sync", xf32.rearrange("p (a d) -> p a d", a=NTT),
                x_h.rearrange("(a p) d -> p a d", p=128))

    vdst = {"td": td8, "tf": tf8, "tmk": tmk8, "tmv": tmv8, "tmr": tmr8}
    vtoks = {nm: dma("gp", vdst[nm],
                     vec_h[nm].rearrange("o (g p) -> (o p) g", p=128),
                     slow=True)
             for nm in vdst}

    E["s"].wait(vtoks["td"])
    E["s"].quiet(lambda e: e.activation(ew8, td8, AF.Exp, bias=0.0, scale=1.0))
    tok_ew = E["s"].op(lambda e: e.activation(ew8, ew8, AF.Exp, bias=0.0,
                                              scale=-1.0))
    E["s"].wait(vtoks["tf"])
    tok_eu = E["s"].op(lambda e: e.activation(eu8, tf8, AF.Exp, bias=0.0,
                                              scale=1.0))

    wgE = [reg(R_QX8 + i * 4 * KB, 4 * KB, F32) for i in range(2)]
    wuE = [reg(R_QX8 + 8 * KB + i * 4 * KB, 4 * KB, F32) for i in range(2)]
    absE = reg(VROW, 4 * KB, F32)
    earlyAB = {}

    def passA_early(nm, src_h_m, bufs, q, acol):
        toks = [None] * 32
        earlyAB[nm] = toks
        for rc in range(32):
            if rc >= 2:
                E[q].wait(toks[rc - 2])
            tok_ld = dma(q, bufs[rc % 2], src_h_m[rc * 128:(rc + 1) * 128, :])
            E["s"].wait(tok_ld)
            toks[rc] = E["s"].op(lambda e, rc=rc, b=bufs[rc % 2]: e.activation(
                absE, b, AF.Abs, bias=0.0, scale=1.0,
                accum_out=asum[:, acol + rc:acol + rc + 1]))
            yield

    gA_wg = passA_early("wg", w_h["wg"], wgE, "sync", 0)
    gA_wu = passA_early("wu", w_h["wu"], wuE, "gp", 32)

    def pump(g, n=1):
        for _ in range(n):
            try:
                next(g)
            except StopIteration:
                return
    pump(gA_wu, 6)

    XN_a = reg(R_RWKV, 32 * KB, F32)
    XN_b = reg(R_XNT, 32 * KB, F32)

    def xn_t(tt):
        base = XN_a if tt < 8 else XN_b
        return base[:, (tt % 8) * 1024:(tt % 8 + 1) * 1024]

    def xf_t(tt):
        return xf32[:, tt * 1024:(tt + 1) * 1024]

    sqscr = reg(POOL, 4 * KB, F32)
    E["v"].wait(tok_x)
    E["s"].wait(tok_x)
    last_s = None
    for tt in range(NTT):
        E["v"].gop(lambda e, tt=tt: e.reduce_sum(S1[:, tt:tt + 1], xf_t(tt),
                                                 axis=AX.X))
        last_s = E["s"].op(lambda e, tt=tt: e.activation(
            sqscr, xf_t(tt), AF.Square, bias=0.0, scale=1.0,
            accum_out=S2[:, tt:tt + 1]))
        pump(gA_wg, 1)
        pump(gA_wu, 1)
    E["v"].wait(last_s)
    E["v"].quiet(lambda e: e.tensor_scalar_mul(mtile, S1, 1.0 / D))
    E["v"].quiet(lambda e: e.tensor_scalar_mul(ex2, S2, 1.0 / D))
    E["v"].quiet(lambda e: e.tensor_mul(rstile, mtile, mtile))
    tok_var = E["v"].op(lambda e: e.tensor_sub(ex2, ex2, rstile))
    E["s"].waits([tok_var, tok_eps])
    tok_sd = E["s"].op(lambda e: e.activation(tqt, ex2, AF.Sqrt, bias=epsb,
                                              scale=1.0))
    E["v"].wait(tok_sd)
    E["v"].gop(lambda e: e.reciprocal(rstile, tqt))
    E["v"].quiet(lambda e: e.tensor_mul(nmrs, mtile, rstile))
    tok_ln = E["v"].op(lambda e: e.tensor_scalar_mul(nmrs, nmrs, -1.0))

    tok_mx = None
    for tt in range(NTT):
        E["v"].quiet(lambda e, tt=tt: e.tensor_scalar(
            xn_t(tt), xf_t(tt), rstile[:, tt:tt + 1], nmrs[:, tt:tt + 1],
            ALU.mult, ALU.add))
        tok_mx = E["v"].gop(lambda e, tt=tt: e.reduce_max(
            mxt[:, tt:tt + 1], xn_t(tt), axis=AX.X, apply_absolute_value=True))

    XNB = reg(POOL, 32 * KB, BF16)
    qi32 = reg(POOL + 32 * KB, 4 * KB, I32)
    qxb = [reg(POOL + 36 * KB, 2 * KB, BF16), reg(POOL + 38 * KB, 2 * KB, BF16)]
    qstg = reg(POOL + 40 * KB, 8 * KB, BF16)

    E["v"].wait(tok_mx)
    E["v"].quiet(lambda e: e.tensor_scalar_max(tqt, mxt, EPS))
    E["v"].gop(lambda e: e.reciprocal(sxt, tqt))
    E["v"].quiet(lambda e: e.tensor_scalar_mul(sxt, sxt, 127.0))
    tok_sx = E["v"].op(lambda e: e.tensor_scalar_mul(inv_s, tqt, 1.0 / 127.0))

    E["te"].wait(tok_ident)
    E["s"].wait(tok_sx)
    qst_prev = None
    for tt in range(NTT):
        tok_r = E["s"].op(lambda e, tt=tt: e.activation(
            qi32, xn_t(tt), AF.Copy, bias=0.0, scale=sxt[:, tt:tt + 1]))
        E["v"].wait(tok_r)
        tok_q = E["v"].op(lambda e, tt=tt: e.tensor_copy(qxb[tt % 2], qi32))
        E["s"].wait(tok_q)
        E["v"].op(lambda e, tt=tt: e.tensor_copy(
            XNB[:, tt * 1024:(tt + 1) * 1024], xn_t(tt)))
        E["te"].wait(tok_q)
        rcs = tt % 4
        qv4 = qstg.rearrange("p (dg rr c) -> p dg rr c", rr=4, c=128)
        for half in range(2):
            bank = 4 + half
            E["te"].wait(ev_guard.get(bank))
            tok_mm = None
            for r in range(4):
                dg = half * 4 + r
                tok_mm = E["te"].op(
                    lambda e, dg=dg, r=r, bank=bank, tt=tt: e.matmul(
                        pa(bank, r * 128, (r + 1) * 128),
                        qxb[tt % 2][:, dg * 128:(dg + 1) * 128], ident,
                        start=True, stop=True))
            dstap = qv4[:, half * 4:(half + 1) * 4, rcs:rcs + 1, :]
            E["v"].waits([tok_mm, qst_prev])
            tok_ev = E["v"].op(lambda e, d=dstap, bank=bank:
                               e.tensor_copy(d, pa(bank)))
            ev_guard[bank] = tok_ev
        if rcs == 3:
            E["s"].wait(tok_ev)
            qst_prev = dma("s", qxT_s[:, :, (tt // 4) * 512:
                                       (tt // 4 + 1) * 512]
                           .transpose([1, 0, 2]), qstg)
        pump(gA_wg, 1)
        pump(gA_wu, 1)
    pump(gA_wg, 32)
    pump(gA_wu, 32)
    tok_xnb_last = (E["v"].sem, E["v"].sem.n)
    tok_r_last = (E["s"].sem, E["s"].sem.n)

    xnT = reg(R_XNT, 32896, BF16)

    def xnt_g(g):
        return xnT[:, g * NXT:(g + 1) * NXT]

    for g in range(NG):
        E["v"].quiet(lambda e, g=g: e.memset(xnt_g(g)[:, 0:1], 0.0))
    tok_z0 = E["v"].op(lambda e: e.memset(xnt_g(0)[:, 0:1], 0.0))
    E["s"].wait(tok_z0)
    E["te"].wait(tok_xnb_last)

    ev_flip = [0]

    def trans_batch(src_aps, dst_ap, bank, eng=None, rhs=None):
        if rhs is None:
            rhs = ident
        E["te"].wait(ev_guard.get(bank))
        E["te"].wait(tok_ident)
        tok_mm = None
        for r, s_ap in enumerate(src_aps):
            tok_mm = E["te"].op(lambda e, s_ap=s_ap, r=r, bank=bank, rhs=rhs:
                                e.matmul(pa(bank, r * 128, (r + 1) * 128),
                                         s_ap, rhs, start=True, stop=True))
        n = len(src_aps) * 128
        if eng is None:
            eng = E["s"] if ev_flip[0] % 2 == 0 else E["v"]
            ev_flip[0] += 1
        eng.wait(tok_mm)
        if eng is E["s"]:
            tok_ev = eng.op(lambda e, d=dst_ap, bank=bank, n=n: e.activation(
                d, pa(bank, 0, n), AF.Copy, bias=0.0, scale=1.0))
        else:
            tok_ev = eng.op(lambda e, d=dst_ap, bank=bank, n=n: e.tensor_copy(
                d, pa(bank, 0, n)))
        ev_guard[bank] = tok_ev
        return tok_ev

    for g in range(NG):
        for q in range(4):
            srcs = [XNB[:, (q * 4 + r) * 1024 + g * 128:
                        (q * 4 + r) * 1024 + (g + 1) * 128] for r in range(4)]
            trans_batch(srcs, xnt_g(g)[:, 1 + q * 512:1 + (q + 1) * 512],
                        4 + q % 4)
    tok_xnbtr_mm = (E["te"].sem, E["te"].sem.n)

    WT = {"wk": reg(R_WT, 16 * KB, BF16),
          "wv": reg(R_WT + 16 * KB, 16 * KB, BF16),
          "wr": reg(R_WT + 32 * KB, 16 * KB, BF16)}
    wbuf = [reg(POOL, 16 * KB, BF16), reg(POOL + 16 * KB, 16 * KB, BF16)]

    def transpose_weight(dst, bi, tok_ld):
        E["te"].wait(tok_ld)
        tok = None
        for dg in range(NG):
            for q in range(2):
                srcs = [wbuf[bi][:, (q * 4 + r) * 1024 + dg * 128:
                                 (q * 4 + r) * 1024 + (dg + 1) * 128]
                        for r in range(4)]
                tok = trans_batch(srcs, dst[:, dg * 1024 + q * 512:
                                            dg * 1024 + (q + 1) * 512],
                                  4 + (dg * 2 + q) % 4)
        return tok

    WoT = reg(R_WT + 48 * KB, 16 * KB, BF16)
    E["gp"].wait(tok_xnbtr_mm)
    ld_k = dma("gp", wbuf[0].rearrange("p (a d) -> p a d", a=NG),
               w_h["wk"].rearrange("(a p) d -> p a d", p=128))
    ld_v = dma("gp", wbuf[1].rearrange("p (a d) -> p a d", a=NG),
               w_h["wv"].rearrange("(a p) d -> p a d", p=128))
    tk = transpose_weight(WT["wk"], 0, ld_k)
    E["gp"].wait(tk)
    ld_r = dma("gp", wbuf[0].rearrange("p (a d) -> p a d", a=NG),
               w_h["wr"].rearrange("(a p) d -> p a d", p=128))
    tv = transpose_weight(WT["wv"], 1, ld_v)
    E["gp"].wait(tv)
    ld_o = dma("gp", wbuf[1].rearrange("p (a d) -> p a d", a=NG),
               w_h["wo"].rearrange("(a p) d -> p a d", p=128))
    transpose_weight(WT["wr"], 0, ld_r)
    transpose_weight(WoT, 1, ld_o)
    wtr_toks = [ev_guard.get(b) for b in (4, 5, 6, 7)]

    rwkvT = reg(R_RWKV, 32 * KB, BF16)
    XM = {"k": reg(POOL, 16 * KB, BF16),
          "v": reg(POOL + 16 * KB, 16 * KB, BF16),
          "r": reg(POOL + 32 * KB, 16 * KB, BF16)}
    TM = {"k": tmk8, "v": tmv8, "r": tmr8}
    Ta = reg(R_QX8, 4 * KB, F32)
    Tb = reg(R_QX8 + 4 * KB, 4 * KB, F32)
    Tv = reg(R_QX8 + 8 * KB, 4 * KB, F32)
    Tek = reg(R_QX8 + 12 * KB, 4 * KB, F32)
    Tz = reg(POOL + 48 * KB, 4 * KB, F32)
    Td = reg(VROW, 4 * KB, F32)
    Tsd = reg(POOL + 52 * KB, 2 * KB, BF16)

    E["v"].waits([tok_ew, tok_eu, tok_r_last, qst_prev] + wtr_toks)
    if not debug:
        E["v"].self_sync = False
    E["te"].waits(wtr_toks)
    E["s"].waits(wtr_toks)
    pend = [None]

    def flush_pend():
        if pend[0] is None:
            return
        tok_rd, tok_sr, g, t0 = pend[0]
        pend[0] = None
        E["v"].waits([tok_rd, tok_sr])
        E["v"].quiet(lambda e: e.tensor_mul(Ta, Tv, Td))
        tok_rw = E["v"].op(lambda e, g=g, t0=t0: e.tensor_mul(
            rwkvT[:, g * T + t0:g * T + t0 + HT], Ta, Tsd))
        E["s"].wait(tok_rw)

    for h in range(2):
        t0 = h * HT
        flush_pend()
        tok_mix = None
        tok_mix_m = {}
        for m in ("k", "v", "r"):
            for g in range(NG):
                E["v"].quiet(lambda e, g=g, t0=t0: e.tensor_sub(
                    Tsd, xnt_g(g)[:, 1 + t0:1 + t0 + HT],
                    xnt_g(g)[:, t0:t0 + HT]))
                tok_mix = E["v"].op(
                    lambda e, m=m, g=g, t0=t0: e.scalar_tensor_tensor(
                        XM[m][:, g * HT:(g + 1) * HT], Tsd, TM[m][:, g:g + 1],
                        xnt_g(g)[:, t0:t0 + HT], ALU.mult, ALU.add))
            tok_mix_m[m] = tok_mix

        def mm_proj(wt_ap, mname, c, bank, g):
            E["te"].wait(tok_mix_m[mname])
            tok = None
            for dg in range(NG):
                tok = E["te"].op(
                    lambda e, dg=dg, c=c, bank=bank, g=g, wt_ap=wt_ap,
                    mname=mname: e.matmul(
                        pa(bank),
                        wt_ap[:, dg * 1024 + g * 128:dg * 1024 + (g + 1) * 128],
                        XM[mname][:, dg * HT + c * 512:dg * HT + (c + 1) * 512],
                        start=(dg == 0), stop=(dg == NG - 1)))
            return tok

        for g in range(NG):
            flush_pend()
            gb = 4 * (g % 2)
            tok_ek = None
            for c in range(2):
                bank = gb + c
                E["te"].wait(ev_guard.get(bank))
                tokk = mm_proj(WT["wk"], "k", c, bank, g)
                E["s"].wait(tokk)
                tok_ek = E["s"].op(lambda e, c=c, bank=bank: e.activation(
                    Tek[:, c * 512:(c + 1) * 512], pa(bank), AF.Exp,
                    bias=0.0, scale=1.0))
                ev_guard[bank] = tok_ek
            tok_v = None
            for c in range(2):
                bank = gb + 2 + c
                E["te"].wait(ev_guard.get(bank))
                tokv = mm_proj(WT["wv"], "v", c, bank, g)
                E["s"].wait(tokv)
                tok_v = E["s"].op(lambda e, c=c, bank=bank: e.activation(
                    Tv[:, c * 512:(c + 1) * 512], pa(bank), AF.Copy,
                    bias=0.0, scale=1.0))
                ev_guard[bank] = tok_v
            tok_sr = None
            for c in range(2):
                bank = gb + c
                E["te"].wait(ev_guard.get(bank))
                tokr = mm_proj(WT["wr"], "r", c, bank, g)
                E["s"].wait(tokr)
                tok_sr = E["s"].op(lambda e, c=c, bank=bank: e.activation(
                    Tsd[:, c * 512:(c + 1) * 512], pa(bank), AF.Sigmoid,
                    bias=0.0, scale=1.0))
                ev_guard[bank] = tok_sr

            eug = eu8[:, g:g + 1]
            ewb = bcast(ew8[:, g:g + 1], HT)
            E["v"].waits([tok_ek, tok_v])
            E["v"].quiet(lambda e: e.tensor_mul(Tz, Tek, Tv))
            if h == 0:
                E["v"].quiet(lambda e, ewb=ewb: e.tensor_tensor_scan(
                    Ta, ewb, Tz, 0.0, ALU.mult, ALU.add))
                E["v"].quiet(lambda e, ewb=ewb: e.tensor_tensor_scan(
                    Tb, ewb, Tek, 0.0, ALU.mult, ALU.add))
                E["v"].quiet(lambda e, eug=eug: e.tensor_scalar_mul(
                    Tv[:, 0:1], Tz[:, 0:1], eug))
            else:
                E["v"].quiet(lambda e, ewb=ewb, g=g: e.tensor_tensor_scan(
                    Ta, ewb, Tz, carryA[:, g:g + 1], ALU.mult, ALU.add))
                E["v"].quiet(lambda e, ewb=ewb, g=g: e.tensor_tensor_scan(
                    Tb, ewb, Tek, carryB[:, g:g + 1], ALU.mult, ALU.add))
                E["v"].quiet(lambda e, eug=eug, g=g: e.scalar_tensor_tensor(
                    Tv[:, 0:1], Tz[:, 0:1], eug, carryA[:, g:g + 1],
                    ALU.mult, ALU.add))
            E["v"].quiet(lambda e, eug=eug: e.scalar_tensor_tensor(
                Tv[:, 1:HT], Tz[:, 1:HT], eug, Ta[:, 0:HT - 1],
                ALU.mult, ALU.add))
            if h == 0:
                E["v"].quiet(lambda e, eug=eug: e.tensor_scalar_mul(
                    Td[:, 0:1], Tek[:, 0:1], eug))
            else:
                E["v"].quiet(lambda e, eug=eug, g=g: e.scalar_tensor_tensor(
                    Td[:, 0:1], Tek[:, 0:1], eug, carryB[:, g:g + 1],
                    ALU.mult, ALU.add))
            tok_den = E["v"].op(lambda e, eug=eug: e.scalar_tensor_tensor(
                Td[:, 1:HT], Tek[:, 1:HT], eug, Tb[:, 0:HT - 1],
                ALU.mult, ALU.add))
            E["v"].quiet(lambda e, g=g: e.tensor_copy(carryA[:, g:g + 1],
                                                      Ta[:, HT - 1:HT]))
            tok_carry = E["v"].op(lambda e, g=g: e.tensor_copy(
                carryB[:, g:g + 1], Tb[:, HT - 1:HT]))
            E["s"].waits([tok_den, tok_carry])
            E["s"].quiet(lambda e: e.activation(Tek, Td, AF.Ln, bias=0.0,
                                                scale=1.0))
            tok_rd = E["s"].op(lambda e: e.activation(Td, Tek, AF.Exp,
                                                      bias=0.0, scale=-1.0))
            pend[0] = (tok_rd, tok_sr, g, t0)

    flush_pend()
    E["v"].self_sync = True
    barrier()

    tok_ones1 = E["v"].op(lambda e: e.memset(ones128f, 1.0))
    E["v"].wait(tok_ones)

    def prep_gu(src_h, dst_s, mwb_dst, q, bank, pbase, acol, veng, e4off,
                ab_toks):
        NB = 4
        wrow = [reg(pbase + i * 4 * KB, 4 * KB, F32) for i in range(NB)]
        qi = reg(pbase + NB * 4 * KB, 4 * KB, I32)
        tern = reg(pbase + NB * 4 * KB + 4 * KB, 2 * KB, BF16)
        est4 = reg(e4off, 8 * KB, BF16)
        nrc = FI // 128
        E["v"].wait(ab_toks[-1])
        tok_as = E["v"].gop(lambda e: e.reduce_sum(
            asum_v[:, :], asum[:, acol:acol + 32], axis=AX.X))
        E["te"].waits([tok_as, tok_ones1, ev_guard.get(bank)])
        tok_ts = E["te"].op(lambda e, bank=bank: e.matmul(
            ps[bank][:, 0:1], ones128f, asum_v, start=True, stop=True))
        E["s"].wait(tok_ts)
        tok_mw = E["s"].op(lambda e, bank=bank: e.activation(
            mwb_dst, ps[bank][:, 0:1], AF.Copy, bias=0.0,
            scale=1.0 / (FI * D)))
        ev_guard[bank] = tok_mw
        E["v"].wait(tok_mw)
        swb_m = sm(372 + (0 if q == "sync" else 1), 1)
        E["v"].quiet(lambda e, swb_m=swb_m: e.tensor_scalar_max(
            swb_m, mwb_dst, EPS))
        tok_bc2 = E["v"].gop(lambda e, swb_m=swb_m: e.reciprocal(swb_m, swb_m))
        q_toks = [None] * nrc
        E[q].wait(ab_toks[-1])
        prev_store = [None, None]
        for rc in range(nrc):
            if rc >= NB:
                E[q].wait(q_toks[rc - NB])
            tok_ld = dma(q, wrow[rc % NB], src_h[rc * 128:(rc + 1) * 128, :])
            if veng == "s":
                E["s"].waits([tok_ld, tok_bc2])
                tok_q = E["s"].op(lambda e, rc=rc: e.activation(
                    qi, wrow[rc % NB], AF.Copy, bias=0.0, scale=swb_m))
            else:
                E["v"].waits([tok_ld, tok_bc2])
                tok_q = E["v"].op(lambda e, rc=rc: e.tensor_scalar(
                    qi, wrow[rc % NB], swb_m, None, ALU.mult))
            q_toks[rc] = tok_q
            E["v"].wait(tok_q)
            tok_t = E["v"].op(lambda e: e.tensor_scalar(
                tern, qi, 1.0, -1.0, ALU.min, ALU.max))
            E["s"].wait(tok_t) if veng == "s" else None
            rcs = rc % 4
            for half in range(2):
                bk = bank if half else bank - 2
                E["te"].waits([tok_t, ev_guard.get(bk)])
                tok_mm = None
                for r in range(4):
                    dg = half * 4 + r
                    tok_mm = E["te"].op(
                        lambda e, dg=dg, r=r, bk=bk: e.matmul(
                            pa(bk, r * 128, (r + 1) * 128),
                            tern[:, dg * 128:(dg + 1) * 128], ident,
                            start=True, stop=True))
                v4 = est4.rearrange("p (dg rr c) -> p dg rr c", rr=4, c=128)
                dstap = v4[:, half * 4:(half + 1) * 4, rcs:rcs + 1, :]
                E[veng].waits([tok_mm, prev_store[0]])
                if veng == "s":
                    tok_ev = E[veng].op(lambda e, d=dstap, bk=bk:
                                        e.activation(d, pa(bk), AF.Copy,
                                                     bias=0.0, scale=1.0))
                else:
                    tok_ev = E[veng].op(lambda e, d=dstap, bk=bk:
                                        e.tensor_copy(d, pa(bk)))
                ev_guard[bk] = tok_ev
            if rcs == 3:
                E["s"].wait(tok_ev)
                prev_store[0] = dma("s", dst_s[rc // 4], est4)
            yield

    def prep_downA():
        wrow_d = reg(R_WT + 8 * KB, 16 * KB, F32)
        absout_d = reg(R_WT + 24 * KB, 16 * KB, F32)
        toks = [None] * 8
        for rc in range(8):
            q = ("sync", "gp")[rc % 2]
            if rc >= 1:
                E[q].wait(toks[rc - 1])
            tok_ld = dma(q, wrow_d, w_h["wd"][rc * 128:(rc + 1) * 128, :])
            E["s"].wait(tok_ld)
            toks[rc] = E["s"].op(lambda e, rc=rc: e.activation(
                absout_d, wrow_d, AF.Abs, bias=0.0, scale=1.0,
                accum_out=asum_d[:, rc:rc + 1]))
            for _ in range(3):
                yield toks

    def prep_down(ab_toks):
        wrow = [reg(POOL, 16 * KB, F32), reg(POOL + 16 * KB, 16 * KB, F32)]
        est4 = reg(R_WT, 32 * KB, BF16)
        qi = reg(R_WT + 32 * KB, 16 * KB, I32)
        absout = reg(R_WT + 32 * KB, 16 * KB, F32)
        tern = reg(POOL + 32 * KB, 8 * KB, BF16)
        nrc = 8
        qs = ["sync", "gp"]
        E["v"].wait(ab_toks[-1])
        tok_as = E["v"].gop(lambda e: e.reduce_sum(
            asum_v[:, :], asum_d[:, 0:nrc], axis=AX.X))
        E["te"].waits([tok_as, tok_ones1, ev_guard.get(6)])
        tok_ts = E["te"].op(lambda e: e.matmul(
            ps[6][:, 0:1], ones128f, asum_v, start=True, stop=True))
        E["s"].wait(tok_ts)
        tok_mw = E["s"].op(lambda e: e.activation(
            mwb["wd"], ps[6][:, 0:1], AF.Copy, bias=0.0, scale=1.0 / (FI * D)))
        ev_guard[6] = tok_mw
        E["v"].wait(tok_mw)
        E["v"].quiet(lambda e: e.tensor_scalar_max(swb, mwb["wd"], EPS))
        tok_bc2 = E["v"].gop(lambda e: e.reciprocal(swb, swb))
        q_toks = [None] * nrc
        E["sync"].wait(ab_toks[-1])
        E["gp"].wait(ab_toks[-1])
        prev_store = [None, None, None, None]
        for rc in range(nrc):
            q = qs[rc % 2]
            if rc >= 2:
                E[q].wait(q_toks[rc - 2])
            tok_ld = dma(q, wrow[rc % 2], w_h["wd"][rc * 128:(rc + 1) * 128, :])
            E["s"].waits([tok_ld, tok_bc2])
            tok_q = E["s"].op(lambda e, rc=rc: e.activation(
                qi, wrow[rc % 2], AF.Copy, bias=0.0, scale=swb))
            q_toks[rc] = tok_q
            E["v"].wait(tok_q)
            tok_t = E["v"].op(lambda e: e.tensor_scalar(
                tern, qi, 1.0, -1.0, ALU.min, ALU.max))
            E["s"].wait(tok_t)
            rcs = rc % 4
            for half in range(8):
                bank = 6 + half % 2
                E["te"].waits([tok_t, ev_guard.get(bank)])
                tok_mm = None
                for r in range(4):
                    ig = half * 4 + r
                    tok_mm = E["te"].op(
                        lambda e, ig=ig, r=r, bank=bank: e.matmul(
                            pa(bank, r * 128, (r + 1) * 128),
                            tern[:, ig * 128:(ig + 1) * 128], ident,
                            start=True, stop=True))
                v4 = est4.rearrange("p (ig rr c) -> p ig rr c", rr=4, c=128)
                dstap = v4[:, half * 4:(half + 1) * 4, rcs:rcs + 1, :]
                E["v"].waits([tok_mm] + prev_store)
                tok_ev = E["v"].op(lambda e, d=dstap, bank=bank:
                                   e.tensor_copy(d, pa(bank)))
                ev_guard[bank] = tok_ev
            if rcs == 3:
                eg = rc // 4
                E["s"].wait(tok_ev)
                for part in range(4):
                    dst = wdT_s[part * 8:(part + 1) * 8, eg, :, :]
                    dst = dst.transpose([1, 0, 2])
                    srcap = est4[:, part * 8 * 512:(part + 1) * 8 * 512]
                    prev_store[part] = dma("s", dst, srcap)

    g1 = prep_gu(w_h["wg"], wgT_s, mwb["wg"], "sync", 6, POOL, 0, "v",
                 POOL + 22 * KB, earlyAB["wg"])
    g2 = prep_gu(w_h["wu"], wuT_s, mwb["wu"], "gp", 7, POOL + 30 * KB, 32, "s",
                 R_WT, earlyAB["wu"])
    g3 = prep_downA()
    dn_toks = [None]
    alive = [g1, g2, g3]
    while alive:
        for gg in list(alive):
            try:
                r = next(gg)
                if gg is g3:
                    dn_toks[0] = r
            except StopIteration:
                alive.remove(gg)
    barrier()
    prep_down(dn_toks[0])
    barrier()
    E["v"].quiet(lambda e: e.tensor_scalar(isg, inv_s, mwb["wg"], None,
                                           ALU.mult))
    tok_is = E["v"].op(lambda e: e.tensor_scalar(isu, inv_s, mwb["wu"], None,
                                                 ALU.mult))
    E["s"].wait(tok_is)

    qxiT = reg(R_XNT, 32 * KB, BF16)
    tok_qxiT = dma("sync", qxiT.rearrange("p (g t) -> p g t", g=NG),
                   qxT_s.transpose([1, 0, 2]))
    E["te"].wait(tok_qxiT)
    barrier()

    wgc = [reg(R_WT, 8 * KB, BF16), reg(R_WT + 8 * KB, 8 * KB, BF16)]
    wuc = [reg(R_WT + 16 * KB, 8 * KB, BF16), reg(R_WT + 24 * KB, 8 * KB, BF16)]
    qhTp = [reg(R_WT + 32 * KB, 16 * KB, BF16),
            reg(POOL + 36 * KB, 16 * KB, BF16)]
    hbuf = reg(POOL, 32 * KB, BF16)
    hi8 = reg(POOL + 32 * KB, 4 * KB, I8)
    fin = [reg(VROW, 2 * KB, F32), reg(VROW + 2 * KB, 2 * KB, F32)]
    xld = [reg(R_QX8, 4 * KB, F32), reg(R_QX8 + 4 * KB, 4 * KB, F32)]
    wdc = [reg(R_QX8 + 8 * KB, 4 * KB, BF16), reg(R_QX8 + 12 * KB, 4 * KB, BF16)]
    f1s = reg(POOL + 52 * KB, 2 * KB, F32)
    dsc = [reg(768 + 2688 + i * 256, 256, BF16) for i in range(4)]
    rgs = reg(VROW, 2 * KB, F32)
    rus = reg(VROW + 2 * KB, 2 * KB, F32)
    sqs = f1s

    n_store = [0]
    store_toks = []
    prev_fin = [None]
    prev_hq_done = [None]
    prev_finV = [None]

    for q4 in range(4):
        tts = [q4 * 4 + i for i in range(4)]
        tq0 = q4 * 512
        hb0_tok = [None]
        hb_toks = {}
        dsc_toks = {}
        E["s"].wait(prev_fin[0])
        E["s"].waits(store_toks[-2:])
        E["v"].wait(prev_hq_done[0])
        tok_wg = {}
        tok_wu = {}
        last_te_wg = {0: ev_guard.get(("wgc", 0)), 1: ev_guard.get(("wgc", 1))}
        tok_h = None
        for ic in range(8):
            bi = ic % 2
            E["sync"].wait(last_te_wg[bi])
            tok_wg[ic] = dma("sync", wgc[bi], wgT_s[ic])
            tok_wu[ic] = dma("sync", wuc[bi], wuT_s[ic])
            for jq in range(4):
                bankg, banku = (jq % 2) * 2, (jq % 2) * 2 + 1
                E["te"].waits([tok_wg[ic], tok_wu[ic],
                               ev_guard.get(bankg), ev_guard.get(banku)])
                tok_g = None
                tok_u = None
                for dg in range(NG):
                    lhs = qxiT[:, dg * T + tq0 + jq * 128:
                               dg * T + tq0 + (jq + 1) * 128]
                    tok_g = E["te"].op(
                        lambda e, lhs=lhs, dg=dg, bi=bi, bankg=bankg:
                        e.matmul(pa(bankg), lhs,
                                 wgc[bi][:, dg * 512:(dg + 1) * 512],
                                 start=(dg == 0), stop=(dg == NG - 1)))
                for dg in range(NG):
                    lhs = qxiT[:, dg * T + tq0 + jq * 128:
                               dg * T + tq0 + (jq + 1) * 128]
                    tok_u = E["te"].op(
                        lambda e, lhs=lhs, dg=dg, bi=bi, banku=banku:
                        e.matmul(pa(banku), lhs,
                                 wuc[bi][:, dg * 512:(dg + 1) * 512],
                                 start=(dg == 0), stop=(dg == NG - 1)))
                last_te_wg[bi] = tok_u
                tcol = tts[jq]
                E["s"].waits([tok_g, tok_u])
                tok_rg = E["s"].op(lambda e, bankg=bankg, tcol=tcol:
                                   e.activation(rgs, pa(bankg), AF.Relu,
                                                bias=0.0,
                                                scale=isg[:, tcol:tcol + 1]))
                ev_guard[bankg] = tok_rg
                tok_ru = E["s"].op(lambda e, banku=banku, tcol=tcol:
                                   e.activation(rus, pa(banku), AF.Copy,
                                                bias=0.0,
                                                scale=isu[:, tcol:tcol + 1]))
                ev_guard[banku] = tok_ru
                tok_sq = E["s"].op(lambda e: e.activation(sqs, rgs, AF.Square,
                                                          bias=0.0, scale=1.0))
                E["v"].waits([tok_sq, tok_ru])
                tok_h = E["v"].op(lambda e, jq=jq, ic=ic: e.tensor_mul(
                    hbuf[:, jq * 4096 + ic * 512:jq * 4096 + (ic + 1) * 512],
                    sqs, rus))
                E["v"].gop(lambda e, jq=jq, ic=ic: e.reduce_max(
                    mhp[jq][:, ic:ic + 1],
                    hbuf[:, jq * 4096 + ic * 512:jq * 4096 + (ic + 1) * 512],
                    axis=AX.X, apply_absolute_value=True))
                E["s"].wait(tok_h)
                if ic == 7 and jq == 0:
                    tt = tts[0]
                    hj = hbuf[:, 0:4096]
                    E["v"].gop(lambda e, tt=tt: e.reduce_max(
                        mh_t[:, tt:tt + 1], mhp[0], axis=AX.X))
                    E["v"].quiet(lambda e, tt=tt: e.tensor_scalar_max(
                        tq2[:, tt:tt + 1], mh_t[:, tt:tt + 1], EPS))
                    E["v"].gop(lambda e, tt=tt: e.reciprocal(
                        sh_t[:, tt:tt + 1], tq2[:, tt:tt + 1]))
                    E["v"].quiet(lambda e, tt=tt: e.tensor_scalar_mul(
                        sh_t[:, tt:tt + 1], sh_t[:, tt:tt + 1], 127.0))
                    E["v"].quiet(lambda e, tt=tt: e.tensor_scalar(
                        scd[:, tt:tt + 1], tq2[:, tt:tt + 1], mwb["wd"],
                        None, ALU.mult))
                    tok_sc0 = E["v"].op(lambda e, tt=tt: e.tensor_scalar_mul(
                        scd[:, tt:tt + 1], scd[:, tt:tt + 1], 1.0 / 127.0))
                    E["s"].wait(tok_sc0)
                    tok_hq0 = E["s"].op(lambda e, hj=hj, tt=tt: e.activation(
                        hi8, hj, AF.Copy, bias=0.0,
                        scale=sh_t[:, tt:tt + 1]))
                    hb0_tok[0] = E["s"].op(lambda e: e.activation(
                        hbuf[:, 0:4096], hi8, AF.Copy, bias=0.0, scale=1.0))
        ev_guard[("wgc", 0)] = last_te_wg[0]
        ev_guard[("wgc", 1)] = last_te_wg[1]

        for jq in range(4):
            tt = tts[jq]
            hj = hbuf[:, jq * 4096:(jq + 1) * 4096]
            if jq == 0:
                hb_toks[0] = hb0_tok[0]
                tok_d0 = E["v"].op(lambda e, tt=tt: e.tensor_scalar(
                    dsc[0], ident, scd[:, tt:tt + 1], None, ALU.mult))
                dsc_toks[0] = tok_d0
                continue
            E["v"].gop(lambda e, jq=jq, tt=tt: e.reduce_max(
                mh_t[:, tt:tt + 1], mhp[jq], axis=AX.X))
            E["v"].quiet(lambda e, tt=tt: e.tensor_scalar_max(
                tq2[:, tt:tt + 1], mh_t[:, tt:tt + 1], EPS))
            E["v"].gop(lambda e, tt=tt: e.reciprocal(
                sh_t[:, tt:tt + 1], tq2[:, tt:tt + 1]))
            E["v"].quiet(lambda e, tt=tt: e.tensor_scalar_mul(
                sh_t[:, tt:tt + 1], sh_t[:, tt:tt + 1], 127.0))
            E["v"].quiet(lambda e, tt=tt: e.tensor_scalar(
                scd[:, tt:tt + 1], tq2[:, tt:tt + 1], mwb["wd"], None,
                ALU.mult))
            tok_sc = E["v"].op(lambda e, tt=tt: e.tensor_scalar_mul(
                scd[:, tt:tt + 1], scd[:, tt:tt + 1], 1.0 / 127.0))
            dsc_toks[jq] = E["v"].op(lambda e, jq=jq, tt=tt: e.tensor_scalar(
                dsc[jq], ident, scd[:, tt:tt + 1], None, ALU.mult))
            E["s"].wait(tok_sc)
            tok_hq = E["s"].op(lambda e, hj=hj, tt=tt: e.activation(
                hi8, hj, AF.Copy, bias=0.0, scale=sh_t[:, tt:tt + 1]))
            qhbf = hbuf[:, jq * 4096:(jq + 1) * 4096]
            hb_toks[jq] = E["s"].op(lambda e, qhbf=qhbf: e.activation(
                qhbf, hi8, AF.Copy, bias=0.0, scale=1.0))

        E["sync"].wait(prev_finV[0])
        xt_toks = [dma("sync", xld[j], x_h[tts[j] * 128:(tts[j] + 1) * 128, :])
                   for j in range(2)]
        xt_toks2 = None

        def att_mms(o):
            pr, j2, ec = o >> 2, (o >> 1) & 1, o & 1
            jq = pr * 2 + j2
            last = None
            for g in range(NG):
                last = E["te"].op(
                    lambda e, g=g, o=o, ec=ec, tq0=tq0, jq=jq:
                    e.matmul(pa(o),
                             rwkvT[:, g * T + tq0 + jq * 128:
                                   g * T + tq0 + (jq + 1) * 128],
                             WoT[:, g * 1024 + ec * 512:
                                 g * 1024 + (ec + 1) * 512],
                             start=(g == 0), stop=False))
            return last

        for o in range(6):
            E["te"].wait(ev_guard.get(o))
            att_mms(o)

        tok_tr_done = None
        for jq in range(4):
            pr, j2 = jq // 2, jq % 2
            qhbf = hbuf[:, jq * 4096:(jq + 1) * 4096]
            E["te"].waits([hb_toks[jq], dsc_toks[jq]])
            for half in range(8):
                bank = 6 + half % 2
                srcs = [qhbf[:, (half * 4 + r) * 128:
                             (half * 4 + r + 1) * 128] for r in range(4)]
                tok_tr_done = trans_batch(
                    srcs, qhTp[pr][:, j2 * 4096 + half * 512:
                                   j2 * 4096 + (half + 1) * 512], bank,
                    eng=E["v"], rhs=dsc[jq])
        prev_hq_done[0] = tok_tr_done

        for o in (6, 7):
            E["te"].wait(ev_guard.get(o))
            att_mms(o)

        tok_dn = {}
        for igp in range(NIG // 2):
            bi = igp % 2
            E["sync"].wait(ev_guard.get(("wdc", bi)))
            tok_wd = dma("sync",
                         wdc[bi].rearrange("p (i g c) -> p i g c", i=2, g=2),
                         wdT_s[igp * 2:igp * 2 + 2].transpose([2, 0, 1, 3]))
            E["te"].waits([tok_wd, tok_tr_done])
            last = None
            for isub in range(2):
                ig = igp * 2 + isub
                for o in range(8):
                    pr, j2, ec = o >> 2, (o >> 1) & 1, o & 1
                    last = E["te"].op(
                        lambda e, pr=pr, j2=j2, ec=ec, ig=ig, bi=bi, o=o,
                        isub=isub:
                        e.matmul(pa(o),
                                 qhTp[pr][:, j2 * 4096 + ig * 128:
                                          j2 * 4096 + (ig + 1) * 128],
                                 wdc[bi][:, isub * 1024 + ec * 512:
                                         isub * 1024 + (ec + 1) * 512],
                                 start=False, stop=(ig == NIG - 1)))
                    if ig == NIG - 1:
                        tok_dn[o] = last
            ev_guard[("wdc", bi)] = last

        for o in range(8):
            pr, j2, ec = o >> 2, (o >> 1) & 1, o & 1
            jq = pr * 2 + j2
            tcol = tts[jq]
            if o == 4:
                E["sync"].wait(prev_fin[0])
                xt_toks2 = [dma("sync", xld[j],
                                x_h[tts[2 + j] * 128:(tts[2 + j] + 1) * 128, :])
                            for j in range(2)]
            E["v"].wait(tok_dn[o])
            E["v"].waits(xt_toks if pr == 0 else xt_toks2)
            if len(store_toks) >= 2:
                E["v"].wait(store_toks[-2])
            tok_fin = E["v"].op(lambda e, o=o, j2=j2, ec=ec: e.tensor_tensor(
                fin[ec], pa(o), xld[j2][:, ec * 512:(ec + 1) * 512], ALU.add))
            ev_guard[o] = tok_fin
            prev_fin[0] = tok_fin
            prev_finV[0] = tok_fin
            E["gp"].wait(tok_fin)
            st_tok = dma("gp", out_h[tcol * 128:(tcol + 1) * 128,
                                     ec * 512:(ec + 1) * 512], fin[ec])
            store_toks.append(st_tok)
            n_store[0] += 1

    for s in QPOOL["gp"]:
        E["gp"].wait((s, s.n))

    with nc.Block() as block:
        @block.sync
        def _(e):
            for f in E["sync"].fns:
                f(e)

        @block.gpsimd
        def _(e):
            for f in E["gp"].fns:
                f(e)

        @block.tensor
        def _(e):
            for f in E["te"].fns:
                f(e)

        @block.vector
        def _(e):
            for f in E["v"].fns:
                f(e)

        @block.scalar
        def _(e):
            for f in E["s"].fns:
                f(e)

    return nc


_NC_CACHE = []


def kernel(L0_attn_x, L0_attn_time_decay, L0_attn_time_first,
           L0_attn_time_mix_k, L0_attn_time_mix_v, L0_attn_time_mix_r,
           L0_attn_Wk, L0_attn_Wv, L0_attn_Wr, L0_attn_Wo,
           L0_ffn_w_gate, L0_ffn_w_up, L0_ffn_w_down):
    x = np.ascontiguousarray(np.asarray(L0_attn_x, dtype=np.float32))
    B = x.shape[0]
    com = {
        "wk": np.ascontiguousarray(np.asarray(L0_attn_Wk, np.float32)),
        "wv": np.ascontiguousarray(np.asarray(L0_attn_Wv, np.float32)),
        "wr": np.ascontiguousarray(np.asarray(L0_attn_Wr, np.float32)),
        "wo": np.ascontiguousarray(np.asarray(L0_attn_Wo, np.float32)),
        "wg": np.ascontiguousarray(np.asarray(L0_ffn_w_gate, np.float32)),
        "wu": np.ascontiguousarray(np.asarray(L0_ffn_w_up, np.float32)),
        "wd": np.ascontiguousarray(np.asarray(L0_ffn_w_down, np.float32)),
        "td": np.ascontiguousarray(
            np.asarray(L0_attn_time_decay, np.float32).reshape(1, D)),
        "tf": np.ascontiguousarray(
            np.asarray(L0_attn_time_first, np.float32).reshape(1, D)),
        "tmk": np.ascontiguousarray(
            np.asarray(L0_attn_time_mix_k, np.float32).reshape(1, D)),
        "tmv": np.ascontiguousarray(
            np.asarray(L0_attn_time_mix_v, np.float32).reshape(1, D)),
        "tmr": np.ascontiguousarray(
            np.asarray(L0_attn_time_mix_r, np.float32).reshape(1, D)),
    }
    if not _NC_CACHE:
        _NC_CACHE.append(build_nc())
    nc = _NC_CACHE[0]
    in_maps = [dict(com, x=np.ascontiguousarray(x[c])) for c in range(B)]
    res = run_bass_kernel_spmd(nc, in_maps, core_ids=list(range(B)))
    return np.stack([np.asarray(r["out"], np.float32) for r in res.results])


# version 35 (final)
# speedup vs baseline: 1.1215x; 1.0206x over previous
import numpy as np
import ml_dtypes
import concourse.bass as bass
import concourse.mybir as mybir
from concourse.bass_utils import run_bass_kernel_spmd

F32 = mybir.dt.float32
BF16 = mybir.dt.bfloat16
I32 = mybir.dt.int32
I8 = mybir.dt.int8
AF = mybir.ActivationFunctionType
ALU = mybir.AluOpType
AX = mybir.AxisListType

T = 2048
D = 1024
FI = 4096
NTT = T // 128
NG = D // 128
NIG = FI // 128
EPS = 1e-5
KB = 1024
HT = 1024
NXT = T + 1


class Sem:
    __slots__ = ("h", "n")

    def __init__(self, h):
        self.h = h
        self.n = 0


class Eng:
    def __init__(self, sem, self_sync=False):
        self.sem = sem
        self.self_sync = self_sync
        self.fns = []
        self.waited = {}

    def wait(self, tok):
        if tok is None:
            return
        sem, val = tok
        if val <= 0 or self.waited.get(id(sem), 0) >= val:
            return
        self.waited[id(sem)] = val
        self.fns.append(lambda e, s=sem, v=val: e.wait_ge(s.h, v))

    def waits(self, toks):
        for t in toks:
            self.wait(t)

    def op(self, fn, inc=1):
        sem = self.sem
        sem.n += inc

        def run(e, f=fn, s=sem, i=inc):
            f(e).then_inc(s.h, i)
        self.fns.append(run)
        tok = (sem, sem.n)
        if self.self_sync:
            self.wait(tok)
        return tok

    def quiet(self, fn):
        if self.self_sync:
            return self.op(lambda e, f=fn: f(e))
        self.fns.append(fn)
        return None

    def gop(self, fn):
        self.wait((self.sem, self.sem.n))
        sem = self.sem
        sem.n += 1

        def run(e, f=fn, s=sem):
            f(e).then_inc(s.h, 1)
        self.fns.append(run)
        tok = (sem, sem.n)
        self.wait(tok)
        return tok


def build_nc(debug=False):
    nc = bass.Bass("TRN2", target_bir_lowering=False,
                   detect_race_conditions=debug)

    x_h = nc.dram_tensor("x", [T, D], F32, kind="ExternalInput")[:]
    out_h = nc.dram_tensor("out", [T, D], F32, kind="ExternalOutput")[:]
    w_h = {}
    for nm, shp in (("wk", [D, D]), ("wv", [D, D]), ("wr", [D, D]),
                    ("wo", [D, D]), ("wg", [FI, D]), ("wu", [FI, D]),
                    ("wd", [D, FI])):
        w_h[nm] = nc.dram_tensor(nm, shp, F32, kind="ExternalInput")[:]
    vec_h = {}
    for nm in ("td", "tf", "tmk", "tmv", "tmr"):
        vec_h[nm] = nc.dram_tensor(nm, [1, D], F32, kind="ExternalInput")[:]
    wgT_s = nc.dram_tensor("wgT_s", [8, 128, FI], BF16, kind="Internal")[:]
    wuT_s = nc.dram_tensor("wuT_s", [8, 128, FI], BF16, kind="Internal")[:]
    wdT_s = nc.dram_tensor("wdT_s", [NIG, 2, 128, 512], BF16, kind="Internal")[:]
    qxT_s = nc.dram_tensor("qxT_s", [NG, 128, T], BF16, kind="Internal")[:]

    ARENA_B = 212800
    BASE = (nc.sbuf_base + 63) & ~63
    esz = {F32: 4, BF16: 2, I32: 4, I8: 1}
    _allocs = {}

    def reg(off_b, nbytes, dt):
        assert off_b % 4 == 0 and off_b + nbytes <= ARENA_B, (off_b, nbytes)
        adt = dt if dt in (F32, BF16) else F32
        key = (off_b, nbytes, adt)
        if key not in _allocs:
            shape = [128, nbytes // esz[adt]]
            _allocs[key] = nc.alloc_sbuf_tensor_at(
                f"rg{off_b}_{nbytes}_{adt}", shape, adt, offset=BASE + off_b)
        h = _allocs[key]
        if dt == adt:
            return h[:, 0:nbytes // esz[dt]]
        return h.bitcast(dt)[:, 0:nbytes // esz[dt]]

    ident = reg(0, 256, BF16)
    ii32 = reg(256, 512, I32)
    ones128f = reg(256, 512, F32)
    smalls = reg(768, 4 * KB, F32)

    def sm(c, n):
        return smalls[:, c:c + n]
    S1, S2 = sm(0, 16), sm(16, 16)
    mtile, ex2 = sm(32, 16), sm(48, 16)
    rstile, nmrs = sm(64, 16), sm(80, 16)
    mxt, tqt = sm(96, 16), sm(112, 16)
    sxt, inv_s = sm(128, 16), sm(144, 16)
    isg, isu = sm(160, 16), sm(176, 16)
    mh_t, tq2 = sm(192, 16), sm(208, 16)
    sh_t, scd = sm(224, 16), sm(240, 16)
    asum = sm(384, 64)
    carryA, carryB = sm(288, 8), sm(296, 8)
    td8, tf8 = sm(304, 8), sm(312, 8)
    ew8, eu8 = sm(320, 8), sm(328, 8)
    tmk8, tmv8, tmr8 = sm(336, 8), sm(344, 8), sm(352, 8)
    ones_col = sm(360, 1)
    mwb = {"wg": sm(361, 1), "wu": sm(362, 1), "wd": sm(363, 1)}
    swb = sm(364, 1)
    mw11 = sm(368, 1)
    rsw11 = sm(369, 1)
    asum_v = sm(370, 1)
    epsb = sm(371, 1)
    mhp = [sm(448, 8), sm(456, 8), sm(464, 8), sm(472, 8)]
    asum_d = sm(480, 8)

    VROW = 4864
    R_XNT = 8960
    R_QX8 = 41856
    R_RWKV = 58240
    R_WT = 91008
    POOL = 156544

    ps = [nc.alloc_psum_tensor(f"psb{i}", [128, 512], F32) for i in range(8)]

    def pa(i, a=0, b=512):
        return ps[i][:, a:b]

    import contextlib
    stack = contextlib.ExitStack()
    nc._keepalive_stack = stack
    sems = {nm: Sem(stack.enter_context(nc.semaphore(nm)))
            for nm in ("gc", "te", "vv", "ss")}
    E = {"sync": Eng(None), "gp": Eng(sems["gc"], self_sync=debug),
         "te": Eng(sems["te"]),
         "v": Eng(sems["vv"], self_sync=True),
         "s": Eng(sems["ss"], self_sync=debug)}
    NQS = 6
    QPOOL = {q: [Sem(stack.enter_context(nc.semaphore(f"{q}{i}")))
                 for i in range(NQS)] for q in ("sync", "gp", "s")}
    QRR = {q: 0 for q in QPOOL}

    def dma(q, out_ap, in_ap, slow=False):
        pool = QPOOL[q]
        s = pool[QRR[q]]
        QRR[q] = (QRR[q] + 1) % NQS
        if s.n > 0:
            E[q].wait((s, s.n))
        s.n += 16

        def run(en, o=out_ap, i=in_ap, ss=s, sl=slow):
            en.dma_start(out=o, in_=i,
                         allow_slow_non_contiguous=sl).then_inc(ss.h, 16)
        E[q].fns.append(run)
        return (s, s.n)

    def barrier():
        toks = [(E[k].sem, E[k].sem.n) for k in E if E[k].sem is not None]
        toks += [(s, s.n) for q in QPOOL for s in QPOOL[q]]
        for k in E:
            E[k].waits(toks)

    def bcast(ap1, n):
        return ap1.broadcast_to((128, n))

    ev_guard = {}

    t_iota = E["gp"].op(lambda e: e.iota(ii32, pattern=[[1, 128]], base=0,
                                         channel_multiplier=-1))
    E["v"].wait(t_iota)
    tok_ident = E["v"].op(lambda e: e.tensor_scalar(ident, ii32, 0.0, None,
                                                    ALU.is_equal))
    tok_ones = E["v"].op(lambda e: e.memset(ones_col, 1.0))
    tok_eps = E["v"].op(lambda e: e.memset(epsb, EPS))

    xf32 = reg(R_WT, 64 * KB, F32)
    tok_x = dma("sync", xf32.rearrange("p (a d) -> p a d", a=NTT),
                x_h.rearrange("(a p) d -> p a d", p=128))

    vdst = {"td": td8, "tf": tf8, "tmk": tmk8, "tmv": tmv8, "tmr": tmr8}
    vtoks = {nm: dma("gp", vdst[nm],
                     vec_h[nm].rearrange("o (g p) -> (o p) g", p=128),
                     slow=True)
             for nm in vdst}

    E["s"].wait(vtoks["td"])
    E["s"].quiet(lambda e: e.activation(ew8, td8, AF.Exp, bias=0.0, scale=1.0))
    tok_ew = E["s"].op(lambda e: e.activation(ew8, ew8, AF.Exp, bias=0.0,
                                              scale=-1.0))
    E["s"].wait(vtoks["tf"])
    tok_eu = E["s"].op(lambda e: e.activation(eu8, tf8, AF.Exp, bias=0.0,
                                              scale=1.0))

    wgE = [reg(R_QX8 + i * 4 * KB, 4 * KB, F32) for i in range(2)]
    wuE = [reg(R_QX8 + 8 * KB + i * 4 * KB, 4 * KB, F32) for i in range(2)]
    absE = reg(VROW, 4 * KB, F32)
    earlyAB = {}

    def passA_early(nm, src_h_m, bufs, q, acol):
        toks = [None] * 32
        earlyAB[nm] = toks
        for rc in range(32):
            if rc >= 2:
                E[q].wait(toks[rc - 2])
            tok_ld = dma(q, bufs[rc % 2], src_h_m[rc * 128:(rc + 1) * 128, :])
            E["s"].wait(tok_ld)
            toks[rc] = E["s"].op(lambda e, rc=rc, b=bufs[rc % 2]: e.activation(
                absE, b, AF.Abs, bias=0.0, scale=1.0,
                accum_out=asum[:, acol + rc:acol + rc + 1]))
            yield

    gA_wg = passA_early("wg", w_h["wg"], wgE, "sync", 0)
    gA_wu = passA_early("wu", w_h["wu"], wuE, "gp", 32)

    def pump(g, n=1):
        for _ in range(n):
            try:
                next(g)
            except StopIteration:
                return
    pump(gA_wu, 6)

    XN_a = reg(R_RWKV, 32 * KB, F32)
    XN_b = reg(R_XNT, 32 * KB, F32)

    def xn_t(tt):
        base = XN_a if tt < 8 else XN_b
        return base[:, (tt % 8) * 1024:(tt % 8 + 1) * 1024]

    def xf_t(tt):
        return xf32[:, tt * 1024:(tt + 1) * 1024]

    sqscr = reg(POOL, 4 * KB, F32)
    E["v"].wait(tok_x)
    E["s"].wait(tok_x)
    last_s = None
    for tt in range(NTT):
        E["v"].gop(lambda e, tt=tt: e.reduce_sum(S1[:, tt:tt + 1], xf_t(tt),
                                                 axis=AX.X))
        last_s = E["s"].op(lambda e, tt=tt: e.activation(
            sqscr, xf_t(tt), AF.Square, bias=0.0, scale=1.0,
            accum_out=S2[:, tt:tt + 1]))
        pump(gA_wg, 1)
        pump(gA_wu, 1)
    E["v"].wait(last_s)
    E["v"].quiet(lambda e: e.tensor_scalar_mul(mtile, S1, 1.0 / D))
    E["v"].quiet(lambda e: e.tensor_scalar_mul(ex2, S2, 1.0 / D))
    E["v"].quiet(lambda e: e.tensor_mul(rstile, mtile, mtile))
    tok_var = E["v"].op(lambda e: e.tensor_sub(ex2, ex2, rstile))
    E["s"].waits([tok_var, tok_eps])
    tok_sd = E["s"].op(lambda e: e.activation(tqt, ex2, AF.Sqrt, bias=epsb,
                                              scale=1.0))
    E["v"].wait(tok_sd)
    E["v"].gop(lambda e: e.reciprocal(rstile, tqt))
    E["v"].quiet(lambda e: e.tensor_mul(nmrs, mtile, rstile))
    tok_ln = E["v"].op(lambda e: e.tensor_scalar_mul(nmrs, nmrs, -1.0))

    tok_mx = None
    for tt in range(NTT):
        E["v"].quiet(lambda e, tt=tt: e.tensor_scalar(
            xn_t(tt), xf_t(tt), rstile[:, tt:tt + 1], nmrs[:, tt:tt + 1],
            ALU.mult, ALU.add))
        tok_mx = E["v"].gop(lambda e, tt=tt: e.reduce_max(
            mxt[:, tt:tt + 1], xn_t(tt), axis=AX.X, apply_absolute_value=True))

    XNB = reg(POOL, 32 * KB, BF16)
    qi32 = reg(POOL + 32 * KB, 4 * KB, I32)
    qxb = [reg(POOL + 36 * KB, 2 * KB, BF16), reg(POOL + 38 * KB, 2 * KB, BF16)]
    qstg = reg(POOL + 40 * KB, 8 * KB, BF16)

    E["v"].wait(tok_mx)
    E["v"].quiet(lambda e: e.tensor_scalar_max(tqt, mxt, EPS))
    E["v"].gop(lambda e: e.reciprocal(sxt, tqt))
    E["v"].quiet(lambda e: e.tensor_scalar_mul(sxt, sxt, 127.0))
    tok_sx = E["v"].op(lambda e: e.tensor_scalar_mul(inv_s, tqt, 1.0 / 127.0))

    E["te"].wait(tok_ident)
    E["s"].wait(tok_sx)
    qst_prev = None
    for tt in range(NTT):
        tok_r = E["s"].op(lambda e, tt=tt: e.activation(
            qi32, xn_t(tt), AF.Copy, bias=0.0, scale=sxt[:, tt:tt + 1]))
        E["v"].wait(tok_r)
        tok_q = E["v"].op(lambda e, tt=tt: e.tensor_copy(qxb[tt % 2], qi32))
        E["s"].wait(tok_q)
        E["v"].op(lambda e, tt=tt: e.tensor_copy(
            XNB[:, tt * 1024:(tt + 1) * 1024], xn_t(tt)))
        E["te"].wait(tok_q)
        rcs = tt % 4
        qv4 = qstg.rearrange("p (dg rr c) -> p dg rr c", rr=4, c=128)
        for half in range(2):
            bank = 4 + half
            E["te"].wait(ev_guard.get(bank))
            tok_mm = None
            for r in range(4):
                dg = half * 4 + r
                tok_mm = E["te"].op(
                    lambda e, dg=dg, r=r, bank=bank, tt=tt: e.matmul(
                        pa(bank, r * 128, (r + 1) * 128),
                        qxb[tt % 2][:, dg * 128:(dg + 1) * 128], ident,
                        start=True, stop=True))
            dstap = qv4[:, half * 4:(half + 1) * 4, rcs:rcs + 1, :]
            E["v"].waits([tok_mm, qst_prev])
            tok_ev = E["v"].op(lambda e, d=dstap, bank=bank:
                               e.tensor_copy(d, pa(bank)))
            ev_guard[bank] = tok_ev
        if rcs == 3:
            E["s"].wait(tok_ev)
            qst_prev = dma("s", qxT_s[:, :, (tt // 4) * 512:
                                       (tt // 4 + 1) * 512]
                           .transpose([1, 0, 2]), qstg)
        pump(gA_wg, 1)
        pump(gA_wu, 1)
    pump(gA_wg, 32)
    pump(gA_wu, 32)
    tok_xnb_last = (E["v"].sem, E["v"].sem.n)
    tok_r_last = (E["s"].sem, E["s"].sem.n)

    xnT = reg(R_XNT, 32896, BF16)

    def xnt_g(g):
        return xnT[:, g * NXT:(g + 1) * NXT]

    for g in range(NG):
        E["v"].quiet(lambda e, g=g: e.memset(xnt_g(g)[:, 0:1], 0.0))
    tok_z0 = E["v"].op(lambda e: e.memset(xnt_g(0)[:, 0:1], 0.0))
    E["s"].wait(tok_z0)
    E["te"].wait(tok_xnb_last)

    ev_flip = [0]

    def trans_batch(src_aps, dst_ap, bank, eng=None, rhs=None):
        if rhs is None:
            rhs = ident
        E["te"].wait(ev_guard.get(bank))
        E["te"].wait(tok_ident)
        tok_mm = None
        for r, s_ap in enumerate(src_aps):
            tok_mm = E["te"].op(lambda e, s_ap=s_ap, r=r, bank=bank, rhs=rhs:
                                e.matmul(pa(bank, r * 128, (r + 1) * 128),
                                         s_ap, rhs, start=True, stop=True))
        n = len(src_aps) * 128
        if eng is None:
            eng = E["s"] if ev_flip[0] % 2 == 0 else E["v"]
            ev_flip[0] += 1
        eng.wait(tok_mm)
        if eng is E["s"]:
            tok_ev = eng.op(lambda e, d=dst_ap, bank=bank, n=n: e.activation(
                d, pa(bank, 0, n), AF.Copy, bias=0.0, scale=1.0))
        else:
            tok_ev = eng.op(lambda e, d=dst_ap, bank=bank, n=n: e.tensor_copy(
                d, pa(bank, 0, n)))
        ev_guard[bank] = tok_ev
        return tok_ev

    for g in range(NG):
        for q in range(4):
            srcs = [XNB[:, (q * 4 + r) * 1024 + g * 128:
                        (q * 4 + r) * 1024 + (g + 1) * 128] for r in range(4)]
            trans_batch(srcs, xnt_g(g)[:, 1 + q * 512:1 + (q + 1) * 512],
                        4 + q % 4)
    tok_xnbtr_mm = (E["te"].sem, E["te"].sem.n)

    WT = {"wk": reg(R_WT, 16 * KB, BF16),
          "wv": reg(R_WT + 16 * KB, 16 * KB, BF16),
          "wr": reg(R_WT + 32 * KB, 16 * KB, BF16)}
    wbuf = [reg(POOL, 16 * KB, BF16), reg(POOL + 16 * KB, 16 * KB, BF16)]

    def transpose_weight(dst, bi, tok_ld):
        E["te"].wait(tok_ld)
        tok = None
        for dg in range(NG):
            for q in range(2):
                srcs = [wbuf[bi][:, (q * 4 + r) * 1024 + dg * 128:
                                 (q * 4 + r) * 1024 + (dg + 1) * 128]
                        for r in range(4)]
                tok = trans_batch(srcs, dst[:, dg * 1024 + q * 512:
                                            dg * 1024 + (q + 1) * 512],
                                  4 + (dg * 2 + q) % 4)
        return tok

    WoT = reg(R_WT + 48 * KB, 16 * KB, BF16)
    E["gp"].wait(tok_xnbtr_mm)
    ld_k = dma("gp", wbuf[0].rearrange("p (a d) -> p a d", a=NG),
               w_h["wk"].rearrange("(a p) d -> p a d", p=128))
    ld_v = dma("gp", wbuf[1].rearrange("p (a d) -> p a d", a=NG),
               w_h["wv"].rearrange("(a p) d -> p a d", p=128))
    tk = transpose_weight(WT["wk"], 0, ld_k)
    E["gp"].wait(tk)
    ld_r = dma("gp", wbuf[0].rearrange("p (a d) -> p a d", a=NG),
               w_h["wr"].rearrange("(a p) d -> p a d", p=128))
    tv = transpose_weight(WT["wv"], 1, ld_v)
    E["gp"].wait(tv)
    ld_o = dma("gp", wbuf[1].rearrange("p (a d) -> p a d", a=NG),
               w_h["wo"].rearrange("(a p) d -> p a d", p=128))
    transpose_weight(WT["wr"], 0, ld_r)
    transpose_weight(WoT, 1, ld_o)
    wtr_toks = [ev_guard.get(b) for b in (4, 5, 6, 7)]

    rwkvT = reg(R_RWKV, 32 * KB, BF16)
    XM = {"k": reg(POOL, 16 * KB, BF16),
          "v": reg(POOL + 16 * KB, 16 * KB, BF16),
          "r": reg(POOL + 32 * KB, 16 * KB, BF16)}
    TM = {"k": tmk8, "v": tmv8, "r": tmr8}
    Ta = reg(R_QX8, 4 * KB, F32)
    Tb = reg(R_QX8 + 4 * KB, 4 * KB, F32)
    Tv = reg(R_QX8 + 8 * KB, 4 * KB, F32)
    Tek = reg(R_QX8 + 12 * KB, 4 * KB, F32)
    Tz = reg(POOL + 48 * KB, 4 * KB, F32)
    Td = reg(VROW, 4 * KB, F32)
    Tsd = reg(POOL + 52 * KB, 2 * KB, BF16)

    E["v"].waits([tok_ew, tok_eu, tok_r_last, qst_prev] + wtr_toks)
    if not debug:
        E["v"].self_sync = False
    E["te"].waits(wtr_toks)
    E["s"].waits(wtr_toks)
    pend = [None]

    def flush_pend():
        if pend[0] is None:
            return
        tok_rd, tok_sr, g, t0 = pend[0]
        pend[0] = None
        E["v"].waits([tok_rd, tok_sr])
        E["v"].quiet(lambda e: e.tensor_mul(Ta, Tv, Td))
        tok_rw = E["v"].op(lambda e, g=g, t0=t0: e.tensor_mul(
            rwkvT[:, g * T + t0:g * T + t0 + HT], Ta, Tsd))
        E["s"].wait(tok_rw)

    for h in range(2):
        t0 = h * HT
        flush_pend()
        tok_mix = None
        tok_mix_m = {}
        for g in range(NG):
            E["v"].quiet(lambda e, g=g, t0=t0: e.tensor_sub(
                XM["r"][:, g * HT:(g + 1) * HT],
                xnt_g(g)[:, 1 + t0:1 + t0 + HT], xnt_g(g)[:, t0:t0 + HT]))
        for m in ("k", "v", "r"):
            for g in range(NG):
                tok_mix = E["v"].op(
                    lambda e, m=m, g=g, t0=t0: e.scalar_tensor_tensor(
                        XM[m][:, g * HT:(g + 1) * HT],
                        XM["r"][:, g * HT:(g + 1) * HT], TM[m][:, g:g + 1],
                        xnt_g(g)[:, t0:t0 + HT], ALU.mult, ALU.add))
            tok_mix_m[m] = tok_mix

        def mm_proj(wt_ap, mname, c, bank, g):
            E["te"].wait(tok_mix_m[mname])
            tok = None
            for dg in range(NG):
                tok = E["te"].op(
                    lambda e, dg=dg, c=c, bank=bank, g=g, wt_ap=wt_ap,
                    mname=mname: e.matmul(
                        pa(bank),
                        wt_ap[:, dg * 1024 + g * 128:dg * 1024 + (g + 1) * 128],
                        XM[mname][:, dg * HT + c * 512:dg * HT + (c + 1) * 512],
                        start=(dg == 0), stop=(dg == NG - 1)))
            return tok

        for g in range(NG):
            flush_pend()
            gb = 4 * (g % 2)
            tok_ek = None
            for c in range(2):
                bank = gb + c
                E["te"].wait(ev_guard.get(bank))
                tokk = mm_proj(WT["wk"], "k", c, bank, g)
                E["s"].wait(tokk)
                tok_ek = E["s"].op(lambda e, c=c, bank=bank: e.activation(
                    Tek[:, c * 512:(c + 1) * 512], pa(bank), AF.Exp,
                    bias=0.0, scale=1.0))
                ev_guard[bank] = tok_ek
            tok_v = None
            for c in range(2):
                bank = gb + 2 + c
                E["te"].wait(ev_guard.get(bank))
                tokv = mm_proj(WT["wv"], "v", c, bank, g)
                E["s"].wait(tokv)
                tok_v = E["s"].op(lambda e, c=c, bank=bank: e.activation(
                    Tv[:, c * 512:(c + 1) * 512], pa(bank), AF.Copy,
                    bias=0.0, scale=1.0))
                ev_guard[bank] = tok_v
            tok_sr = None
            for c in range(2):
                bank = gb + c
                E["te"].wait(ev_guard.get(bank))
                tokr = mm_proj(WT["wr"], "r", c, bank, g)
                E["s"].wait(tokr)
                tok_sr = E["s"].op(lambda e, c=c, bank=bank: e.activation(
                    Tsd[:, c * 512:(c + 1) * 512], pa(bank), AF.Sigmoid,
                    bias=0.0, scale=1.0))
                ev_guard[bank] = tok_sr

            eug = eu8[:, g:g + 1]
            ewb = bcast(ew8[:, g:g + 1], HT)
            E["v"].waits([tok_ek, tok_v])
            E["v"].quiet(lambda e: e.tensor_mul(Tz, Tek, Tv))
            if h == 0:
                E["v"].quiet(lambda e, ewb=ewb: e.tensor_tensor_scan(
                    Ta, ewb, Tz, 0.0, ALU.mult, ALU.add))
                E["v"].quiet(lambda e, ewb=ewb: e.tensor_tensor_scan(
                    Tb, ewb, Tek, 0.0, ALU.mult, ALU.add))
                E["v"].quiet(lambda e, eug=eug: e.tensor_scalar_mul(
                    Tv[:, 0:1], Tz[:, 0:1], eug))
            else:
                E["v"].quiet(lambda e, ewb=ewb, g=g: e.tensor_tensor_scan(
                    Ta, ewb, Tz, carryA[:, g:g + 1], ALU.mult, ALU.add))
                E["v"].quiet(lambda e, ewb=ewb, g=g: e.tensor_tensor_scan(
                    Tb, ewb, Tek, carryB[:, g:g + 1], ALU.mult, ALU.add))
                E["v"].quiet(lambda e, eug=eug, g=g: e.scalar_tensor_tensor(
                    Tv[:, 0:1], Tz[:, 0:1], eug, carryA[:, g:g + 1],
                    ALU.mult, ALU.add))
            E["v"].quiet(lambda e, eug=eug: e.scalar_tensor_tensor(
                Tv[:, 1:HT], Tz[:, 1:HT], eug, Ta[:, 0:HT - 1],
                ALU.mult, ALU.add))
            if h == 0:
                E["v"].quiet(lambda e, eug=eug: e.tensor_scalar_mul(
                    Td[:, 0:1], Tek[:, 0:1], eug))
            else:
                E["v"].quiet(lambda e, eug=eug, g=g: e.scalar_tensor_tensor(
                    Td[:, 0:1], Tek[:, 0:1], eug, carryB[:, g:g + 1],
                    ALU.mult, ALU.add))
            tok_den = E["v"].op(lambda e, eug=eug: e.scalar_tensor_tensor(
                Td[:, 1:HT], Tek[:, 1:HT], eug, Tb[:, 0:HT - 1],
                ALU.mult, ALU.add))
            E["v"].quiet(lambda e, g=g: e.tensor_copy(carryA[:, g:g + 1],
                                                      Ta[:, HT - 1:HT]))
            tok_carry = E["v"].op(lambda e, g=g: e.tensor_copy(
                carryB[:, g:g + 1], Tb[:, HT - 1:HT]))
            E["s"].waits([tok_den, tok_carry])
            E["s"].quiet(lambda e: e.activation(Tek, Td, AF.Ln, bias=0.0,
                                                scale=1.0))
            tok_rd = E["s"].op(lambda e: e.activation(Td, Tek, AF.Exp,
                                                      bias=0.0, scale=-1.0))
            pend[0] = (tok_rd, tok_sr, g, t0)

    flush_pend()
    E["v"].self_sync = True
    barrier()

    tok_ones1 = E["v"].op(lambda e: e.memset(ones128f, 1.0))
    E["v"].wait(tok_ones)

    def prep_gu(src_h, dst_s, mwb_dst, q, bank, pbase, acol, veng, e4off,
                ab_toks):
        NB = 4
        wrow = [reg(pbase + i * 4 * KB, 4 * KB, F32) for i in range(NB)]
        qi = reg(pbase + NB * 4 * KB, 4 * KB, I32)
        tern = reg(pbase + NB * 4 * KB + 4 * KB, 2 * KB, BF16)
        est4 = reg(e4off, 8 * KB, BF16)
        nrc = FI // 128
        E["v"].wait(ab_toks[-1])
        tok_as = E["v"].gop(lambda e: e.reduce_sum(
            asum_v[:, :], asum[:, acol:acol + 32], axis=AX.X))
        E["te"].waits([tok_as, tok_ones1, ev_guard.get(bank)])
        tok_ts = E["te"].op(lambda e, bank=bank: e.matmul(
            ps[bank][:, 0:1], ones128f, asum_v, start=True, stop=True))
        E["s"].wait(tok_ts)
        tok_mw = E["s"].op(lambda e, bank=bank: e.activation(
            mwb_dst, ps[bank][:, 0:1], AF.Copy, bias=0.0,
            scale=1.0 / (FI * D)))
        ev_guard[bank] = tok_mw
        E["v"].wait(tok_mw)
        swb_m = sm(372 + (0 if q == "sync" else 1), 1)
        E["v"].quiet(lambda e, swb_m=swb_m: e.tensor_scalar_max(
            swb_m, mwb_dst, EPS))
        tok_bc2 = E["v"].gop(lambda e, swb_m=swb_m: e.reciprocal(swb_m, swb_m))
        q_toks = [None] * nrc
        E[q].wait(ab_toks[-1])
        prev_store = [None, None]
        for rc in range(nrc):
            if rc >= NB:
                E[q].wait(q_toks[rc - NB])
            tok_ld = dma(q, wrow[rc % NB], src_h[rc * 128:(rc + 1) * 128, :])
            if veng == "s":
                E["s"].waits([tok_ld, tok_bc2])
                tok_q = E["s"].op(lambda e, rc=rc: e.activation(
                    qi, wrow[rc % NB], AF.Copy, bias=0.0, scale=swb_m))
            else:
                E["v"].waits([tok_ld, tok_bc2])
                tok_q = E["v"].op(lambda e, rc=rc: e.tensor_scalar(
                    qi, wrow[rc % NB], swb_m, None, ALU.mult))
            q_toks[rc] = tok_q
            E["v"].wait(tok_q)
            tok_t = E["v"].op(lambda e: e.tensor_scalar(
                tern, qi, 1.0, -1.0, ALU.min, ALU.max))
            E["s"].wait(tok_t) if veng == "s" else None
            rcs = rc % 4
            for half in range(2):
                bk = bank if half else bank - 2
                E["te"].waits([tok_t, ev_guard.get(bk)])
                tok_mm = None
                for r in range(4):
                    dg = half * 4 + r
                    tok_mm = E["te"].op(
                        lambda e, dg=dg, r=r, bk=bk: e.matmul(
                            pa(bk, r * 128, (r + 1) * 128),
                            tern[:, dg * 128:(dg + 1) * 128], ident,
                            start=True, stop=True))
                v4 = est4.rearrange("p (dg rr c) -> p dg rr c", rr=4, c=128)
                dstap = v4[:, half * 4:(half + 1) * 4, rcs:rcs + 1, :]
                E[veng].waits([tok_mm, prev_store[0]])
                if veng == "s":
                    tok_ev = E[veng].op(lambda e, d=dstap, bk=bk:
                                        e.activation(d, pa(bk), AF.Copy,
                                                     bias=0.0, scale=1.0))
                else:
                    tok_ev = E[veng].op(lambda e, d=dstap, bk=bk:
                                        e.tensor_copy(d, pa(bk)))
                ev_guard[bk] = tok_ev
            if rcs == 3:
                E["s"].wait(tok_ev)
                prev_store[0] = dma("s", dst_s[rc // 4], est4)
            yield

    def prep_downA():
        wrow_d = reg(R_WT + 8 * KB, 16 * KB, F32)
        absout_d = reg(R_WT + 24 * KB, 16 * KB, F32)
        toks = [None] * 8
        for rc in range(8):
            q = ("sync", "gp")[rc % 2]
            if rc >= 1:
                E[q].wait(toks[rc - 1])
            tok_ld = dma(q, wrow_d, w_h["wd"][rc * 128:(rc + 1) * 128, :])
            E["s"].wait(tok_ld)
            toks[rc] = E["s"].op(lambda e, rc=rc: e.activation(
                absout_d, wrow_d, AF.Abs, bias=0.0, scale=1.0,
                accum_out=asum_d[:, rc:rc + 1]))
            for _ in range(3):
                yield toks

    def prep_down(ab_toks):
        wrow = [reg(POOL, 16 * KB, F32), reg(POOL + 16 * KB, 16 * KB, F32)]
        est4 = reg(R_WT, 32 * KB, BF16)
        qi = reg(R_WT + 32 * KB, 16 * KB, I32)
        absout = reg(R_WT + 32 * KB, 16 * KB, F32)
        tern = reg(POOL + 32 * KB, 8 * KB, BF16)
        nrc = 8
        qs = ["sync", "gp"]
        E["v"].wait(ab_toks[-1])
        tok_as = E["v"].gop(lambda e: e.reduce_sum(
            asum_v[:, :], asum_d[:, 0:nrc], axis=AX.X))
        E["te"].waits([tok_as, tok_ones1, ev_guard.get(6)])
        tok_ts = E["te"].op(lambda e: e.matmul(
            ps[6][:, 0:1], ones128f, asum_v, start=True, stop=True))
        E["s"].wait(tok_ts)
        tok_mw = E["s"].op(lambda e: e.activation(
            mwb["wd"], ps[6][:, 0:1], AF.Copy, bias=0.0, scale=1.0 / (FI * D)))
        ev_guard[6] = tok_mw
        E["v"].wait(tok_mw)
        E["v"].quiet(lambda e: e.tensor_scalar_max(swb, mwb["wd"], EPS))
        tok_bc2 = E["v"].gop(lambda e: e.reciprocal(swb, swb))
        q_toks = [None] * nrc
        E["sync"].wait(ab_toks[-1])
        E["gp"].wait(ab_toks[-1])
        prev_store = [None, None, None, None]
        for rc in range(nrc):
            q = qs[rc % 2]
            if rc >= 2:
                E[q].wait(q_toks[rc - 2])
            tok_ld = dma(q, wrow[rc % 2], w_h["wd"][rc * 128:(rc + 1) * 128, :])
            E["s"].waits([tok_ld, tok_bc2])
            tok_q = E["s"].op(lambda e, rc=rc: e.activation(
                qi, wrow[rc % 2], AF.Copy, bias=0.0, scale=swb))
            q_toks[rc] = tok_q
            E["v"].wait(tok_q)
            tok_t = E["v"].op(lambda e: e.tensor_scalar(
                tern, qi, 1.0, -1.0, ALU.min, ALU.max))
            E["s"].wait(tok_t)
            rcs = rc % 4
            for half in range(8):
                bank = 6 + half % 2
                E["te"].waits([tok_t, ev_guard.get(bank)])
                tok_mm = None
                for r in range(4):
                    ig = half * 4 + r
                    tok_mm = E["te"].op(
                        lambda e, ig=ig, r=r, bank=bank: e.matmul(
                            pa(bank, r * 128, (r + 1) * 128),
                            tern[:, ig * 128:(ig + 1) * 128], ident,
                            start=True, stop=True))
                v4 = est4.rearrange("p (ig rr c) -> p ig rr c", rr=4, c=128)
                dstap = v4[:, half * 4:(half + 1) * 4, rcs:rcs + 1, :]
                E["v"].waits([tok_mm] + prev_store)
                tok_ev = E["v"].op(lambda e, d=dstap, bank=bank:
                                   e.tensor_copy(d, pa(bank)))
                ev_guard[bank] = tok_ev
            if rcs == 3:
                eg = rc // 4
                E["s"].wait(tok_ev)
                for part in range(4):
                    dst = wdT_s[part * 8:(part + 1) * 8, eg, :, :]
                    dst = dst.transpose([1, 0, 2])
                    srcap = est4[:, part * 8 * 512:(part + 1) * 8 * 512]
                    prev_store[part] = dma("s", dst, srcap)

    g1 = prep_gu(w_h["wg"], wgT_s, mwb["wg"], "sync", 6, POOL, 0, "v",
                 POOL + 22 * KB, earlyAB["wg"])
    g2 = prep_gu(w_h["wu"], wuT_s, mwb["wu"], "gp", 7, POOL + 30 * KB, 32, "s",
                 R_WT, earlyAB["wu"])
    g3 = prep_downA()
    dn_toks = [None]
    alive = [g1, g2, g3]
    while alive:
        for gg in list(alive):
            try:
                r = next(gg)
                if gg is g3:
                    dn_toks[0] = r
            except StopIteration:
                alive.remove(gg)
    barrier()
    prep_down(dn_toks[0])
    barrier()
    E["v"].quiet(lambda e: e.tensor_scalar(isg, inv_s, mwb["wg"], None,
                                           ALU.mult))
    tok_is = E["v"].op(lambda e: e.tensor_scalar(isu, inv_s, mwb["wu"], None,
                                                 ALU.mult))
    E["s"].wait(tok_is)

    qxiT = reg(R_XNT, 32 * KB, BF16)
    tok_qxiT = dma("sync", qxiT.rearrange("p (g t) -> p g t", g=NG),
                   qxT_s.transpose([1, 0, 2]))
    E["te"].wait(tok_qxiT)
    barrier()

    wgc = [reg(R_WT, 8 * KB, BF16), reg(R_WT + 8 * KB, 8 * KB, BF16)]
    wuc = [reg(R_WT + 16 * KB, 8 * KB, BF16), reg(R_WT + 24 * KB, 8 * KB, BF16)]
    qhTp = [reg(R_WT + 32 * KB, 16 * KB, BF16),
            reg(POOL + 36 * KB, 16 * KB, BF16)]
    hbuf = reg(POOL, 32 * KB, BF16)
    hi8 = reg(POOL + 32 * KB, 4 * KB, I8)
    fin = [reg(VROW, 2 * KB, F32), reg(VROW + 2 * KB, 2 * KB, F32)]
    xld = [reg(R_QX8, 4 * KB, F32), reg(R_QX8 + 4 * KB, 4 * KB, F32)]
    wdc = [reg(R_QX8 + 8 * KB, 4 * KB, BF16), reg(R_QX8 + 12 * KB, 4 * KB, BF16)]
    f1s = reg(POOL + 52 * KB, 2 * KB, F32)
    dsc = [reg(768 + 2688 + i * 256, 256, BF16) for i in range(4)]
    rgs = reg(VROW, 2 * KB, F32)
    rus = reg(VROW + 2 * KB, 2 * KB, F32)
    sqs = f1s

    n_store = [0]
    store_toks = []
    prev_fin = [None]
    prev_hq_done = [None]
    prev_finV = [None]

    for q4 in range(4):
        tts = [q4 * 4 + i for i in range(4)]
        tq0 = q4 * 512
        hb0_tok = [None]
        hb_toks = {}
        dsc_toks = {}
        E["s"].wait(prev_fin[0])
        E["s"].waits(store_toks[-2:])
        E["v"].wait(prev_hq_done[0])
        tok_wg = {}
        tok_wu = {}
        last_te_wg = {0: ev_guard.get(("wgc", 0)), 1: ev_guard.get(("wgc", 1))}
        tok_h = None
        for ic in range(8):
            bi = ic % 2
            E["sync"].wait(last_te_wg[bi])
            tok_wg[ic] = dma("sync", wgc[bi], wgT_s[ic])
            tok_wu[ic] = dma("sync", wuc[bi], wuT_s[ic])
            for jq in range(4):
                bankg, banku = (jq % 2) * 2, (jq % 2) * 2 + 1
                E["te"].waits([tok_wg[ic], tok_wu[ic],
                               ev_guard.get(bankg), ev_guard.get(banku)])
                tok_g = None
                tok_u = None
                for dg in range(NG):
                    lhs = qxiT[:, dg * T + tq0 + jq * 128:
                               dg * T + tq0 + (jq + 1) * 128]
                    tok_g = E["te"].op(
                        lambda e, lhs=lhs, dg=dg, bi=bi, bankg=bankg:
                        e.matmul(pa(bankg), lhs,
                                 wgc[bi][:, dg * 512:(dg + 1) * 512],
                                 start=(dg == 0), stop=(dg == NG - 1)))
                for dg in range(NG):
                    lhs = qxiT[:, dg * T + tq0 + jq * 128:
                               dg * T + tq0 + (jq + 1) * 128]
                    tok_u = E["te"].op(
                        lambda e, lhs=lhs, dg=dg, bi=bi, banku=banku:
                        e.matmul(pa(banku), lhs,
                                 wuc[bi][:, dg * 512:(dg + 1) * 512],
                                 start=(dg == 0), stop=(dg == NG - 1)))
                last_te_wg[bi] = tok_u
                tcol = tts[jq]
                E["s"].waits([tok_g, tok_u])
                tok_rg = E["s"].op(lambda e, bankg=bankg, tcol=tcol:
                                   e.activation(rgs, pa(bankg), AF.Relu,
                                                bias=0.0,
                                                scale=isg[:, tcol:tcol + 1]))
                ev_guard[bankg] = tok_rg
                tok_ru = E["s"].op(lambda e, banku=banku, tcol=tcol:
                                   e.activation(rus, pa(banku), AF.Copy,
                                                bias=0.0,
                                                scale=isu[:, tcol:tcol + 1]))
                ev_guard[banku] = tok_ru
                tok_sq = E["s"].op(lambda e: e.activation(sqs, rgs, AF.Square,
                                                          bias=0.0, scale=1.0))
                E["v"].waits([tok_sq, tok_ru])
                tok_h = E["v"].op(lambda e, jq=jq, ic=ic: e.tensor_mul(
                    hbuf[:, jq * 4096 + ic * 512:jq * 4096 + (ic + 1) * 512],
                    sqs, rus))
                E["v"].gop(lambda e, jq=jq, ic=ic: e.reduce_max(
                    mhp[jq][:, ic:ic + 1],
                    hbuf[:, jq * 4096 + ic * 512:jq * 4096 + (ic + 1) * 512],
                    axis=AX.X, apply_absolute_value=True))
                E["s"].wait(tok_h)
                if ic == 7 and jq == 0:
                    tt = tts[0]
                    hj = hbuf[:, 0:4096]
                    E["v"].gop(lambda e, tt=tt: e.reduce_max(
                        mh_t[:, tt:tt + 1], mhp[0], axis=AX.X))
                    E["v"].quiet(lambda e, tt=tt: e.tensor_scalar_max(
                        tq2[:, tt:tt + 1], mh_t[:, tt:tt + 1], EPS))
                    E["v"].gop(lambda e, tt=tt: e.reciprocal(
                        sh_t[:, tt:tt + 1], tq2[:, tt:tt + 1]))
                    E["v"].quiet(lambda e, tt=tt: e.tensor_scalar_mul(
                        sh_t[:, tt:tt + 1], sh_t[:, tt:tt + 1], 127.0))
                    E["v"].quiet(lambda e, tt=tt: e.tensor_scalar(
                        scd[:, tt:tt + 1], tq2[:, tt:tt + 1], mwb["wd"],
                        None, ALU.mult))
                    tok_sc0 = E["v"].op(lambda e, tt=tt: e.tensor_scalar_mul(
                        scd[:, tt:tt + 1], scd[:, tt:tt + 1], 1.0 / 127.0))
                    E["s"].wait(tok_sc0)
                    tok_hq0 = E["s"].op(lambda e, hj=hj, tt=tt: e.activation(
                        hi8, hj, AF.Copy, bias=0.0,
                        scale=sh_t[:, tt:tt + 1]))
                    hb0_tok[0] = E["s"].op(lambda e: e.activation(
                        hbuf[:, 0:4096], hi8, AF.Copy, bias=0.0, scale=1.0))
        ev_guard[("wgc", 0)] = last_te_wg[0]
        ev_guard[("wgc", 1)] = last_te_wg[1]

        for jq in range(4):
            tt = tts[jq]
            hj = hbuf[:, jq * 4096:(jq + 1) * 4096]
            if jq == 0:
                hb_toks[0] = hb0_tok[0]
                tok_d0 = E["v"].op(lambda e, tt=tt: e.tensor_scalar(
                    dsc[0], ident, scd[:, tt:tt + 1], None, ALU.mult))
                dsc_toks[0] = tok_d0
                continue
            E["v"].gop(lambda e, jq=jq, tt=tt: e.reduce_max(
                mh_t[:, tt:tt + 1], mhp[jq], axis=AX.X))
            E["v"].quiet(lambda e, tt=tt: e.tensor_scalar_max(
                tq2[:, tt:tt + 1], mh_t[:, tt:tt + 1], EPS))
            E["v"].gop(lambda e, tt=tt: e.reciprocal(
                sh_t[:, tt:tt + 1], tq2[:, tt:tt + 1]))
            E["v"].quiet(lambda e, tt=tt: e.tensor_scalar_mul(
                sh_t[:, tt:tt + 1], sh_t[:, tt:tt + 1], 127.0))
            E["v"].quiet(lambda e, tt=tt: e.tensor_scalar(
                scd[:, tt:tt + 1], tq2[:, tt:tt + 1], mwb["wd"], None,
                ALU.mult))
            tok_sc = E["v"].op(lambda e, tt=tt: e.tensor_scalar_mul(
                scd[:, tt:tt + 1], scd[:, tt:tt + 1], 1.0 / 127.0))
            dsc_toks[jq] = E["v"].op(lambda e, jq=jq, tt=tt: e.tensor_scalar(
                dsc[jq], ident, scd[:, tt:tt + 1], None, ALU.mult))
            E["s"].wait(tok_sc)
            tok_hq = E["s"].op(lambda e, hj=hj, tt=tt: e.activation(
                hi8, hj, AF.Copy, bias=0.0, scale=sh_t[:, tt:tt + 1]))
            qhbf = hbuf[:, jq * 4096:(jq + 1) * 4096]
            hb_toks[jq] = E["s"].op(lambda e, qhbf=qhbf: e.activation(
                qhbf, hi8, AF.Copy, bias=0.0, scale=1.0))

        E["sync"].wait(prev_finV[0])
        xt_toks = [dma("sync", xld[j], x_h[tts[j] * 128:(tts[j] + 1) * 128, :])
                   for j in range(2)]
        xt_toks2 = None

        def att_mms(o):
            pr, j2, ec = o >> 2, (o >> 1) & 1, o & 1
            jq = pr * 2 + j2
            last = None
            for g in range(NG):
                last = E["te"].op(
                    lambda e, g=g, o=o, ec=ec, tq0=tq0, jq=jq:
                    e.matmul(pa(o),
                             rwkvT[:, g * T + tq0 + jq * 128:
                                   g * T + tq0 + (jq + 1) * 128],
                             WoT[:, g * 1024 + ec * 512:
                                 g * 1024 + (ec + 1) * 512],
                             start=(g == 0), stop=False))
            return last

        for o in range(6):
            E["te"].wait(ev_guard.get(o))
            att_mms(o)

        tok_tr_done = None
        for jq in range(4):
            pr, j2 = jq // 2, jq % 2
            qhbf = hbuf[:, jq * 4096:(jq + 1) * 4096]
            E["te"].waits([hb_toks[jq], dsc_toks[jq]])
            for half in range(8):
                bank = 6 + half % 2
                srcs = [qhbf[:, (half * 4 + r) * 128:
                             (half * 4 + r + 1) * 128] for r in range(4)]
                tok_tr_done = trans_batch(
                    srcs, qhTp[pr][:, j2 * 4096 + half * 512:
                                   j2 * 4096 + (half + 1) * 512], bank,
                    eng=E["v"], rhs=dsc[jq])
        prev_hq_done[0] = tok_tr_done

        for o in (6, 7):
            E["te"].wait(ev_guard.get(o))
            att_mms(o)

        tok_dn = {}
        for igp in range(NIG // 2):
            bi = igp % 2
            E["sync"].wait(ev_guard.get(("wdc", bi)))
            tok_wd = dma("sync",
                         wdc[bi].rearrange("p (i g c) -> p i g c", i=2, g=2),
                         wdT_s[igp * 2:igp * 2 + 2].transpose([2, 0, 1, 3]))
            E["te"].waits([tok_wd, tok_tr_done])
            last = None
            for isub in range(2):
                ig = igp * 2 + isub
                for o in range(8):
                    pr, j2, ec = o >> 2, (o >> 1) & 1, o & 1
                    last = E["te"].op(
                        lambda e, pr=pr, j2=j2, ec=ec, ig=ig, bi=bi, o=o,
                        isub=isub:
                        e.matmul(pa(o),
                                 qhTp[pr][:, j2 * 4096 + ig * 128:
                                          j2 * 4096 + (ig + 1) * 128],
                                 wdc[bi][:, isub * 1024 + ec * 512:
                                         isub * 1024 + (ec + 1) * 512],
                                 start=False, stop=(ig == NIG - 1)))
                    if ig == NIG - 1:
                        tok_dn[o] = last
            ev_guard[("wdc", bi)] = last

        for o in range(8):
            pr, j2, ec = o >> 2, (o >> 1) & 1, o & 1
            jq = pr * 2 + j2
            tcol = tts[jq]
            if o == 4:
                E["sync"].wait(prev_fin[0])
                xt_toks2 = [dma("sync", xld[j],
                                x_h[tts[2 + j] * 128:(tts[2 + j] + 1) * 128, :])
                            for j in range(2)]
            E["v"].wait(tok_dn[o])
            E["v"].waits(xt_toks if pr == 0 else xt_toks2)
            if len(store_toks) >= 2:
                E["v"].wait(store_toks[-2])
            tok_fin = E["v"].op(lambda e, o=o, j2=j2, ec=ec: e.tensor_tensor(
                fin[ec], pa(o), xld[j2][:, ec * 512:(ec + 1) * 512], ALU.add))
            ev_guard[o] = tok_fin
            prev_fin[0] = tok_fin
            prev_finV[0] = tok_fin
            E["gp"].wait(tok_fin)
            st_tok = dma("gp", out_h[tcol * 128:(tcol + 1) * 128,
                                     ec * 512:(ec + 1) * 512], fin[ec])
            store_toks.append(st_tok)
            n_store[0] += 1

    for s in QPOOL["gp"]:
        E["gp"].wait((s, s.n))

    with nc.Block() as block:
        @block.sync
        def _(e):
            for f in E["sync"].fns:
                f(e)

        @block.gpsimd
        def _(e):
            for f in E["gp"].fns:
                f(e)

        @block.tensor
        def _(e):
            for f in E["te"].fns:
                f(e)

        @block.vector
        def _(e):
            for f in E["v"].fns:
                f(e)

        @block.scalar
        def _(e):
            for f in E["s"].fns:
                f(e)

    return nc


_NC_CACHE = []


def kernel(L0_attn_x, L0_attn_time_decay, L0_attn_time_first,
           L0_attn_time_mix_k, L0_attn_time_mix_v, L0_attn_time_mix_r,
           L0_attn_Wk, L0_attn_Wv, L0_attn_Wr, L0_attn_Wo,
           L0_ffn_w_gate, L0_ffn_w_up, L0_ffn_w_down):
    x = np.ascontiguousarray(np.asarray(L0_attn_x, dtype=np.float32))
    B = x.shape[0]
    com = {
        "wk": np.ascontiguousarray(np.asarray(L0_attn_Wk, np.float32)),
        "wv": np.ascontiguousarray(np.asarray(L0_attn_Wv, np.float32)),
        "wr": np.ascontiguousarray(np.asarray(L0_attn_Wr, np.float32)),
        "wo": np.ascontiguousarray(np.asarray(L0_attn_Wo, np.float32)),
        "wg": np.ascontiguousarray(np.asarray(L0_ffn_w_gate, np.float32)),
        "wu": np.ascontiguousarray(np.asarray(L0_ffn_w_up, np.float32)),
        "wd": np.ascontiguousarray(np.asarray(L0_ffn_w_down, np.float32)),
        "td": np.ascontiguousarray(
            np.asarray(L0_attn_time_decay, np.float32).reshape(1, D)),
        "tf": np.ascontiguousarray(
            np.asarray(L0_attn_time_first, np.float32).reshape(1, D)),
        "tmk": np.ascontiguousarray(
            np.asarray(L0_attn_time_mix_k, np.float32).reshape(1, D)),
        "tmv": np.ascontiguousarray(
            np.asarray(L0_attn_time_mix_v, np.float32).reshape(1, D)),
        "tmr": np.ascontiguousarray(
            np.asarray(L0_attn_time_mix_r, np.float32).reshape(1, D)),
    }
    if not _NC_CACHE:
        _NC_CACHE.append(build_nc())
    nc = _NC_CACHE[0]
    in_maps = [dict(com, x=np.ascontiguousarray(x[c])) for c in range(B)]
    res = run_bass_kernel_spmd(nc, in_maps, core_ids=list(range(B)))
    return np.stack([np.asarray(r["out"], np.float32) for r in res.results])
